# version 08
# baseline (speedup 1.0000x reference)
import numpy as np
import concourse.bass as bass
import concourse.mybir as mybir
from concourse.bass_utils import run_bass_kernel_spmd

F32 = mybir.dt.float32
BF16 = mybir.dt.bfloat16
AF = mybir.ActivationFunctionType
ALU = mybir.AluOpType
AX = mybir.AxisListType.X

E = 1024
H = 16
DH = 64
NH = 6
FH = 4096
ISQ32 = float(1.0 / np.sqrt(np.float32(32.0)))
EPS = 1e-5


class Rec:

    def __init__(self, name):
        self.name = name
        self.calls = []

    def __getattr__(self, op):
        if op.startswith("_"):
            raise AttributeError(op)
        calls = self.calls

        def call(*a, **k):
            ent = {"op": op, "a": a, "k": k, "chain": []}
            calls.append(ent)

            class Hnd:
                def then_inc(self, sem, val=1):
                    ent["chain"].append((sem, val))
                    return self
            return Hnd()
        return call


class Sem:
    def __init__(self, nc, name):
        self._cm = nc.semaphore(name)
        self.h = self._cm.__enter__()
        self.n = 0

    def inc(self, _hnd, k=1):
        _hnd.then_inc(self.h, k)
        self.n += k
        return self.n


def build_program(nseq, S, debug=False):
    nc = bass.Bass("TRN2", target_bir_lowering=False)
    NT = nseq * S
    QT = S // 128
    KC = S // 128
    GR = S // 256
    NHEAD = nseq * H
    NQG = NHEAD * QT
    TFF = min(1024, NT)
    NFT = NT // TFF
    NSUB = TFF // 128
    NHALF = TFF // 512

    x_in = nc.dram_tensor("x", [nseq, S, E], F32, kind="ExternalInput")
    wqT_d = nc.dram_tensor("wqT", [128, 8, E], F32, kind="ExternalInput")
    wvT_d = nc.dram_tensor("wvT", [128, 8, E], F32, kind="ExternalInput")
    bq_d = nc.dram_tensor("bq", [E], F32, kind="ExternalInput")
    bv_d = nc.dram_tensor("bv", [E], F32, kind="ExternalInput")
    hyp_d = nc.dram_tensor("hyp", [65, NH], F32, kind="ExternalInput")
    w1T_d = nc.dram_tensor("w1T", [32, 128, 8, 128], F32, kind="ExternalInput")
    w2T_d = nc.dram_tensor("w2T", [32, 128, 8, 128], F32, kind="ExternalInput")
    h1T_d = nc.dram_tensor("h1T", [32, 128, 32, 128], F32, kind="ExternalInput")
    h2T_d = nc.dram_tensor("h2T", [32, 128, 32, 128], F32, kind="ExternalInput")
    w3T_d = nc.dram_tensor("w3T", [8, 128, 32, 128], F32, kind="ExternalInput")
    s1_d = nc.dram_tensor("s1", [FH], F32, kind="ExternalInput")
    s2_d = nc.dram_tensor("s2", [FH], F32, kind="ExternalInput")
    h1s_d = nc.dram_tensor("h1s", [FH], F32, kind="ExternalInput")
    h2s_d = nc.dram_tensor("h2s", [FH], F32, kind="ExternalInput")
    w3s_d = nc.dram_tensor("w3s", [E], F32, kind="ExternalInput")
    out_d = nc.dram_tensor("out", [nseq, S, E], F32, kind="ExternalOutput")
    qsc = nc.dram_tensor("qsc", [nseq, 8, 128, S], BF16)
    vsc = nc.dram_tensor("vsc", [nseq, H, 128, KC, DH], BF16)
    attsc = nc.dram_tensor("attsc", [nseq, S, E], F32)
    x1sc = nc.dram_tensor("x1sc", [nseq, S, E], F32)

    cms = []

    def sbuf(name, shape, dt):
        cm = nc.sbuf_tensor(name, shape, dt)
        t = cm.__enter__()
        cms.append(cm)
        return t

    def psum(name, shape, dt=F32):
        cm = nc.psum_tensor(name, shape, dt)
        t = cm.__enter__()
        cms.append(cm)
        return t

    ptg3 = sbuf("ptg3", [128, 4096], BF16)
    iota_p = sbuf("iota_p", [128, 1], F32)
    hyp_sb = sbuf("hyp_sb", [64, NH], BF16)
    hb_col = sbuf("hb_col", [NH, 1], F32)
    pw2 = sbuf("pw2", [NH, 1], BF16)
    ones_col = sbuf("ones_col", [128, 1], BF16)
    neg63_col = sbuf("neg63_col", [128, 1], BF16)
    ones_row = sbuf("ones_row", [1, 128], BF16)
    bq_sb = sbuf("bq_sb", [128, 8], F32)
    bv_sb = sbuf("bv_sb", [128, 8], F32)
    s1_sb = sbuf("s1_sb", [128, 32], F32)
    s2_sb = sbuf("s2_sb", [128, 32], F32)
    h1s_sb = sbuf("h1s_sb", [128, 32], F32)
    h2s_sb = sbuf("h2s_sb", [128, 32], F32)
    w3s_sb = sbuf("w3s_sb", [128, 8], F32)
    lncol = sbuf("lncol", [128, 8], F32)
    biascol_sb = sbuf("biascol_sb", [128, max(QT, 1)], F32)
    denbuf = sbuf("denbuf", [128, max(QT, 4)], F32)
    denr = sbuf("denr", [128, max(QT, 4)], F32)
    bits_sb = sbuf("bits_sb", [NH, 512], BF16)
    bucket_sb = sbuf("bucket_sb", [1, 512], BF16)
    bqall = sbuf("bqall", [128, max(QT, 1)], F32)

    big0 = sbuf("big0", [128, 32768], BF16)
    big1 = sbuf("big1", [128, 32768], BF16)
    wq_sb = sbuf("wq_sb", [128, 8192], BF16)
    wv_sb = sbuf("wv_sb", [128, 8192], BF16)
    hst_sb = [sbuf("hsta", [128, 4096], BF16), sbuf("hstb", [128, 4096], BF16)]
    w12_sb = [hst_sb[0][:, 0:2048], hst_sb[1][:, 0:2048]]
    ab_sb = sbuf("ab_sb", [128, 1024], BF16)
    f32ar = sbuf("f32ar", [128, 4096], F32)

    ptg = [big0[:, 0:4096], big0[:, 4096:8192], ptg3[:, :]]
    qbuf = [big0[:, 8192:12288], big0[:, 12288:16384]]
    zbuf = [big0[:, 16384:20480], big0[:, 20480:24576]]
    pbuf = [big0[:, 24576:28672], big0[:, 28672:32768]]
    sq_sb = big1[:, 0:4096]
    bcb_sb = big1[:, 4096:8192]
    ctile = big1[:, 8192:12288]
    vbuf = [big1[:, 12288:14336], big1[:, 14336:16384]]
    xhat_tok = big1[:, 16384:18432]
    xhat_fm = big1[:, 18432:20480]
    vfm_grp = big1[:, 20480:22528]
    vtok_grp = big1[:, 22528:24576]
    qev = [big1[:, 24576:24832], big1[:, 24832:25088]]
    h_buf = big0
    g1_buf = big1
    xh2_fm = wq_sb
    w3fm = wv_sb
    xbuf = f32ar[:, 0:2048]
    xsq = f32ar[:, 2048:3072]
    attsl = f32ar[:, 3584:3840]
    f32a = f32ar[:, 0:1024]
    f32b = f32ar[:, 1024:2048]
    f32c = f32ar[:, 3072:4096]

    ps_a = psum("ps_a", [128, 2048])
    ps_b = psum("ps_b", [128, 1024])
    ps_c = psum("ps_c", [128, 512])
    ps_d = psum("ps_d", [128, 512])

    sT = Sem(nc, "sT")
    sV = Sem(nc, "sV")
    sS = Sem(nc, "sS")
    sG = Sem(nc, "sG")
    sX = Sem(nc, "sX")
    sDW = Sem(nc, "sDW")
    sDS = Sem(nc, "sDS")
    sSC = Sem(nc, "sSC")
    sOUT = Sem(nc, "sOUT")

    T, V, S_, X, G = Rec("T"), Rec("V"), Rec("S"), Rec("X"), Rec("G")

    G.iota(iota_p[:, :], [[0, 1]], channel_multiplier=1,
           allow_small_or_imprecise_dtypes=True)
    G.memset(ones_col[:, :], 1.0)
    G.memset(neg63_col[:, :], -63.0)
    g_init = sG.inc(G.memset(ones_row[:, :], 1.0))

    G.dma_start(wq_sb[:, :], bass.AP(wqT_d, 0, [[8 * E, 128], [1, 8 * E]])).then_inc(sDW.h, 16)
    sDW.n += 16
    G.dma_start(wv_sb[:, :], bass.AP(wvT_d, 0, [[8 * E, 128], [1, 8 * E]])).then_inc(sDW.h, 16)
    sDW.n += 16
    w_qv = sDW.n
    for t_, src_ in ((hyp_sb, hyp_d[0:64, :]),
                     (hb_col, hyp_d[64:65, :].rearrange("a n -> n a")),
                     (bq_sb, bq_d[:].rearrange("(c p) -> p c", p=128)),
                     (bv_sb, bv_d[:].rearrange("(c p) -> p c", p=128)),
                     (s1_sb, s1_d[:].rearrange("(c p) -> p c", p=128)),
                     (s2_sb, s2_d[:].rearrange("(c p) -> p c", p=128)),
                     (h1s_sb, h1s_d[:].rearrange("(c p) -> p c", p=128)),
                     (h2s_sb, h2s_d[:].rearrange("(c p) -> p c", p=128)),
                     (w3s_sb, w3s_d[:].rearrange("(c p) -> p c", p=128))):
        G.dma_start(t_[:, :], src_, allow_slow_non_contiguous=True).then_inc(sDW.h, 16)
        sDW.n += 16
    w_misc = sDW.n

    V.wait_ge(sG.h, g_init)
    V.wait_ge(sDW.h, w_misc)
    V.tensor_scalar(bq_sb[:, :], bq_sb[:, :], ISQ32, None, ALU.mult)
    v_init = sV.inc(V.tensor_scalar(hb_col[:, :], hb_col[:, :], ISQ32, None, ALU.mult))
    T.wait_ge(sV.h, v_init)
    S_.wait_ge(sV.h, v_init)
    S_.wait_ge(sG.h, g_init)
    sS.inc(S_.activation(pw2[0:NH, :], iota_p[0:NH, :], AF.Exp,
                         scale=float(np.log(2.0))))

    st = {}

    def vstep(hnd):
        n = sV.inc(hnd)
        V.wait_ge(sV.h, n)
        return n

    def emit_ln(src_ap, dst_ap, pre_wait_sv=None):
        if pre_wait_sv is not None:
            S_.wait_ge(sV.h, pre_wait_sv)
        if st.get("ln_rd") is not None:
            S_.wait_ge(sV.h, st["ln_rd"])
        st["ln_sq"] = sS.inc(S_.activation(xsq[:, :], src_ap, AF.Square))
        V.wait_ge(sS.h, st["ln_sq"])
        V.reduce_sum(lncol[:, 0:1], src_ap, axis=AX)
        vstep(V.reduce_sum(lncol[:, 1:2], xsq[:, :], axis=AX))
        vstep(V.tensor_scalar(lncol[:, 2:3], lncol[:, 0:1], 1.0 / E, None, ALU.mult))
        vstep(V.tensor_tensor(lncol[:, 3:4], lncol[:, 2:3], lncol[:, 2:3], ALU.mult))
        vstep(V.tensor_scalar(lncol[:, 4:5], lncol[:, 1:2], 1.0 / E, lncol[:, 3:4],
                              ALU.mult, ALU.subtract))
        st["ln_var"] = sV.inc(
            V.tensor_scalar(lncol[:, 4:5], lncol[:, 4:5], EPS, None, ALU.add))
        st["ln_rd"] = st["ln_var"]
        S_.wait_ge(sV.h, st["ln_var"])
        st["ln_sd"] = sS.inc(S_.activation(lncol[:, 5:6], lncol[:, 4:5], AF.Sqrt))
        V.wait_ge(sS.h, st["ln_sd"])
        vstep(V.reciprocal(lncol[:, 6:7], lncol[:, 5:6]))
        st["ln_out"] = sV.inc(
            V.tensor_scalar(dst_ap, src_ap, lncol[:, 2:3], lncol[:, 6:7],
                            ALU.subtract, ALU.mult))
        return st["ln_out"]

    for s in range(nseq):
        for g in range(GR):
            t0 = g * 256
            if st.get("a1_xcons") is not None:
                S_.wait_ge(sV.h, st["a1_xcons"])
            src = x_in[s, t0:t0 + 256, :].rearrange("(j p) f -> p j f", p=128)
            S_.dma_start(xbuf[:, :].rearrange("p (j f) -> p j f", j=2), src).then_inc(sDS.h, 16)
            sDS.n += 16
            st["a1_xr"] = sDS.n
            for j in range(2):
                S_.wait_ge(sDS.h, st["a1_xr"])
                ln_out = emit_ln(xbuf[:, j * E:(j + 1) * E],
                                 xhat_tok[:, j * E:(j + 1) * E])
                if j == 1:
                    st["a1_xcons"] = ln_out
            X.wait_ge(sV.h, st["ln_out"])
            if st.get("a1_projdone") is not None:
                X.wait_ge(sT.h, st["a1_projdone"])
            for j in range(2):
                X.dma_start_transpose(
                    xhat_fm.rearrange("p (ic j c) -> p ic j c", ic=8, j=2)[:, :, j, :],
                    xhat_tok[:, j * E:(j + 1) * E]
                    .rearrange("p (ic c) -> p ic c", ic=8)).then_inc(sX.h, 16)
                sX.n += 16
            st["a1_xfm"] = sX.n

            for wi, (w_sb, bias_sb, scl) in enumerate(
                    ((wq_sb, bq_sb, ISQ32), (wv_sb, bv_sb, 1.0))):
                for oc in range(8):
                    slot_off = wi * 1024 + (oc % 2) * 512
                    pst = ps_a[:, slot_off:slot_off + 256]
                    T.wait_ge(sX.h, st["a1_xfm"])
                    if wi == 0 and oc == 0:
                        T.wait_ge(sDW.h, w_qv)
                    key = f"a1_sl{wi}{oc % 2}"
                    if st.get(key) is not None:
                        T.wait_ge(sV.h, st[key])
                    for ic in range(8):
                        hnd = T.matmul(pst,
                                       w_sb[:, ic * E + oc * 128: ic * E + (oc + 1) * 128],
                                       xhat_fm[:, ic * 256:(ic + 1) * 256],
                                       start=(ic == 0), stop=(ic == 7))
                    st[f"a1_mm{wi}"] = sT.inc(hnd)
                    if wi == 1 and oc == 7:
                        st["a1_projdone"] = sT.n
                    V.wait_ge(sT.h, st[f"a1_mm{wi}"])
                    if wi == 0:
                        qv = qev[oc % 2]
                        if st.get(f"a1_qst{oc % 2}") is not None:
                            V.wait_ge(sSC.h, st[f"a1_qst{oc % 2}"])
                        st[key] = sV.inc(
                            V.tensor_scalar(qv[:, :], pst,
                                            bias_sb[:, oc:oc + 1], scl,
                                            ALU.add, ALU.mult))
                        S_.wait_ge(sV.h, st[key])
                        S_.dma_start(qsc[s, oc, :, t0:t0 + 256], qv[:, :]).then_inc(sSC.h, 16)
                        sSC.n += 16
                        st[f"a1_qst{oc % 2}"] = sSC.n
                    else:
                        if st.get("a1_vtr") is not None and oc == 0:
                            V.wait_ge(sX.h, st["a1_vtr"])
                        st[key] = sV.inc(
                            V.tensor_scalar(
                                vfm_grp.rearrange("p (tj oc c) -> p oc tj c",
                                                  tj=2, oc=8)[:, oc, :, :],
                                pst.rearrange("p (tj c) -> p tj c", tj=2),
                                bias_sb[:, oc:oc + 1], None, ALU.add))
                        st["a1_ev1"] = st[key]
            X.wait_ge(sV.h, st["a1_ev1"])
            if st.get("a1_vst") is not None:
                X.wait_ge(sDS.h, st["a1_vst"])
            for tj in range(2):
                X.dma_start_transpose(
                    vtok_grp[:, tj * 1024:(tj + 1) * 1024]
                    .rearrange("p (oc c) -> p oc c", oc=8),
                    vfm_grp[:, tj * 1024:(tj + 1) * 1024]
                    .rearrange("p (oc c) -> p oc c", oc=8)).then_inc(sX.h, 16)
                sX.n += 16
            st["a1_vtr"] = sX.n
            S_.wait_ge(sX.h, st["a1_vtr"])
            for tj in range(2):
                dst = bass.AP(vsc, s * H * 128 * KC * DH + (g * 2 + tj) * DH,
                              [[KC * DH, 128], [128 * KC * DH, 16], [1, DH]])
                S_.dma_start(dst, vtok_grp[:, tj * 1024:(tj + 1) * 1024]
                             .rearrange("p (hh d) -> p hh d", hh=16)).then_inc(sDS.h, 16)
                sDS.n += 16
            st["a1_vst"] = sDS.n

    endA1 = dict(T=sT.n, V=sV.n, S=sS.n, X=sX.n, SC=sSC.n, DS=sDS.n)

    for qg in range(NQG + 8):
        def emit_pv(qp):
            hip = qp // QT
            bufp = hip % 2
            T.wait_ge(sX.h, st[f"a2_ptr{qp % 3}"])
            if st.get(f"a2_evac{qp % 4}") is not None:
                T.wait_ge(sS.h, st[f"a2_evac{qp % 4}"])
            for m in range(KC):
                hnd = T.matmul(ps_d[:, (qp % 4) * 64:(qp % 4) * 64 + 64],
                               ptg[qp % 3][:, m * 128:(m + 1) * 128],
                               vbuf[bufp][:, m * DH:(m + 1) * DH],
                               start=(m == 0), stop=(m == KC - 1))
            st[f"a2_pv{qp % 4}"] = sT.inc(hnd)
            if qp % QT == QT - 1:
                st[f"a2_vfree{bufp}"] = sT.n

        if 0 <= qg - 1 < NQG:
            emit_pv(qg - 1)

        if qg < NQG:
            hi = qg // QT
            qt = qg % QT
            s, h = hi // H, hi % H
            buf = hi % 2
            zt = qg % 2
            if qt == 0:
                if hi == 0:
                    G.wait_ge(sSC.h, endA1["SC"])
                    G.wait_ge(sDS.h, endA1["DS"])
                if st.get(f"a2_qfree{buf}") is not None:
                    G.wait_ge(sT.h, st[f"a2_qfree{buf}"])
                if st.get(f"a2_vfree{buf}") is not None:
                    G.wait_ge(sT.h, st[f"a2_vfree{buf}"])
                G.dma_start(qbuf[buf][0:64, :],
                            qsc[s, h // 2, 64 * (h % 2):64 * (h % 2) + 64, :]).then_inc(sDS.h, 16)
                sDS.n += 16
                G.dma_start(vbuf[buf],
                            bass.AP(vsc, (s * H + h) * 128 * KC * DH,
                                    [[KC * DH, 128], [1, KC * DH]])).then_inc(sDS.h, 16)
                sDS.n += 16
                st[f"a2_qv{buf}"] = sDS.n
                S_.wait_ge(sDS.h, st[f"a2_qv{buf}"])
                if st.get("a2_sqfree") is not None:
                    S_.wait_ge(sT.h, st["a2_sqfree"])
                st["a2_sq"] = sS.inc(S_.activation(sq_sb[0:64, :], qbuf[buf][0:64, :],
                                                   AF.Square))
                T.wait_ge(sS.h, st["a2_sq"])
                if st.get("a2_bcp") is not None:
                    T.wait_ge(sV.h, st["a2_bcp"])
                for qq in range(QT):
                    hnd = T.matmul(ps_d[:, 256 + qq:257 + qq],
                                   sq_sb[0:64, qq * 128:(qq + 1) * 128],
                                   neg63_col[0:64, :], start=True, stop=True)
                st["a2_s2b"] = sT.inc(hnd)
                V.wait_ge(sT.h, st["a2_s2b"])
                if st.get("a2_expl") is not None:
                    V.wait_ge(sS.h, st["a2_expl"])
                st["a2_bcp"] = sV.inc(V.tensor_copy(biascol_sb[:, :],
                                                    ps_d[:, 256:256 + QT]))
                for ch in range(S // 512):
                    T.wait_ge(sDS.h, st[f"a2_qv{buf}"])
                    if hi == 0 and ch == 0:
                        T.wait_ge(sS.h, endA1["S"])
                        T.wait_ge(sV.h, endA1["V"])
                    if ch == 0 and st.get("a2_z3") is not None:
                        T.wait_ge(sV.h, st["a2_z3"])
                    if st.get("a2_bits") is not None:
                        T.wait_ge(sV.h, st["a2_bits"])
                    hnd = T.matmul(ps_b[0:NH, 0:512], hyp_sb[:, :],
                                   qbuf[buf][0:64, ch * 512:(ch + 1) * 512],
                                   start=True, stop=True)
                    st["a2_bp"] = sT.inc(hnd)
                    V.wait_ge(sT.h, st["a2_bp"])
                    if ch == 0 and st.get("a2_ctG") is not None:
                        V.wait_ge(sG.h, st["a2_ctG"])
                    if st.get("a2_sum") is not None:
                        V.wait_ge(sT.h, st["a2_sum"])
                    st["a2_bits"] = sV.inc(
                        V.tensor_scalar(bits_sb[:, :], ps_b[0:NH, 0:512],
                                        hb_col[:, 0:1], 0.0, ALU.add, ALU.is_ge))
                    T.wait_ge(sV.h, st["a2_bits"])
                    if st.get("a2_bcbev") is not None:
                        T.wait_ge(sV.h, st["a2_bcbev"])
                    hnd = T.matmul(ps_c[0:1, 0:512], pw2[:, :],
                                   bits_sb[:, :], start=True, stop=True)
                    if st.get("a2_bqc") is not None:
                        T.wait_ge(sV.h, st["a2_bqc"])
                    for i4 in range(4):
                        hnd = T.matmul(ps_d[:, 288 + i4:289 + i4],
                                       bits_sb[:, i4 * 128:(i4 + 1) * 128], pw2[:, :],
                                       start=True, stop=True)
                    st["a2_sum"] = sT.inc(hnd)
                    V.wait_ge(sT.h, st["a2_sum"])
                    V.tensor_copy(bucket_sb[0:1, :], ps_c[0:1, 0:512])
                    st["a2_bqc"] = sV.inc(
                        V.tensor_copy(bqall[:, ch * 4:(ch + 1) * 4], ps_d[:, 288:292]))
                    T.wait_ge(sV.h, st["a2_bqc"])
                    hnd = T.matmul(ps_c[0:128, 0:512], ones_row[0:1, :],
                                   bucket_sb[0:1, :], start=True, stop=True)
                    st["a2_bc"] = sT.inc(hnd)
                    V.wait_ge(sT.h, st["a2_bc"])
                    st["a2_bcbev"] = sV.inc(
                        V.tensor_copy(bcb_sb[:, ch * 512:(ch + 1) * 512],
                                      ps_c[0:128, 0:512]))
                st["a2_bcb"] = st["a2_bcbev"]

            V.wait_ge(sV.h, st["a2_bcb"])
            st["a2_ct"] = ("V", sV.inc(
                V.tensor_scalar(ctile[:, :], bcb_sb[:, :], bqall[:, qt:qt + 1],
                                62.0, ALU.is_equal, ALU.add)))

            if st.get("a2_z1") is not None:
                T.wait_ge(sV.h, st["a2_z1"])
            for kk in range(4):
                hnd = T.matmul(ps_a[:, kk * 512:(kk + 1) * 512],
                               qbuf[buf][0:64, qt * 128:(qt + 1) * 128],
                               qbuf[buf][0:64, kk * 512:(kk + 1) * 512],
                               start=True, stop=True)
            st["a2_qkA"] = sT.inc(hnd)
            if qt == 0:
                T.wait_ge(sV.h, st["a2_bits"])
            if st.get("a2_z2") is not None:
                T.wait_ge(sV.h, st["a2_z2"])
            for kk in range(2):
                hnd = T.matmul(ps_b[:, kk * 512:(kk + 1) * 512],
                               qbuf[buf][0:64, qt * 128:(qt + 1) * 128],
                               qbuf[buf][0:64, 2048 + kk * 512:2048 + (kk + 1) * 512],
                               start=True, stop=True)
            st["a2_qkB"] = sT.inc(hnd)

            ctl, ctn = st["a2_ct"]
            if ctl == "G":
                V.wait_ge(sG.h, ctn)
            else:
                V.wait_ge(sV.h, ctn)
            V.wait_ge(sT.h, st["a2_qkA"])
            if st.get(f"a2_expz{zt}") is not None:
                V.wait_ge(sS.h, st[f"a2_expz{zt}"])
            st["a2_z1"] = sV.inc(
                V.tensor_tensor(zbuf[zt][:, 0:2048], ctile[:, 0:2048],
                                ps_a[:, 0:2048], ALU.mult))
            V.wait_ge(sT.h, st["a2_qkB"])
            st["a2_z2"] = sV.inc(
                V.tensor_tensor(zbuf[zt][:, 2048:3072], ctile[:, 2048:3072],
                                ps_b[:, 0:1024], ALU.mult))

            T.wait_ge(sV.h, st["a2_z2"])
            for kk in range(2):
                hnd = T.matmul(ps_b[:, kk * 512:(kk + 1) * 512],
                               qbuf[buf][0:64, qt * 128:(qt + 1) * 128],
                               qbuf[buf][0:64, 3072 + kk * 512:3072 + (kk + 1) * 512],
                               start=True, stop=True)
            st["a2_qkB2"] = sT.inc(hnd)
            if qt == QT - 1:
                st[f"a2_qfree{buf}"] = sT.n
                st["a2_sqfree"] = sT.n

            V.wait_ge(sT.h, st["a2_qkB2"])
            st["a2_z3"] = sV.inc(
                V.tensor_tensor(zbuf[zt][:, 3072:4096], ctile[:, 3072:4096],
                                ps_b[:, 0:1024], ALU.mult))

        if qg % 4 == 0 and 0 < qg <= NQG:
            lo = qg - 4
            V.wait_ge(sS.h, st[f"a2_expn{(qg - 1) % 2}"])
            st["a2_den"] = sV.inc(
                V.reciprocal(denr[:, lo % QT:lo % QT + 4],
                             denbuf[:, lo % QT:lo % QT + 4]))

        if 0 <= qg - 4 < NQG:
            qe = qg - 4
            S_.wait_ge(sT.h, st[f"a2_pv{qe % 4}"])
            S_.wait_ge(sV.h, st["a2_den"])
            if qe % 4 == 0 and st.get("a2_store") is not None:
                S_.wait_ge(sSC.h, st["a2_store"])
            hnd = S_.activation(attsl[:, (qe % 4) * 64:(qe % 4) * 64 + 64],
                                ps_d[:, (qe % 4) * 64:(qe % 4) * 64 + 64],
                                AF.Copy, scale=denr[:, qe % QT:qe % QT + 1])
            st[f"a2_evac{qe % 4}"] = sS.inc(hnd)
            if qe % 4 == 3:
                G.wait_ge(sS.h, st[f"a2_evac{qe % 4}"])
                hie = qe // QT
                se, he = hie // H, hie % H
                qs = (qe % QT) // 4
                dst = bass.AP(attsc, (se * S + qs * 512) * E + he * DH,
                              [[E, 128], [128 * E, 4], [1, DH]])
                G.dma_start(dst, attsl[:, :]
                            .rearrange("p (j d) -> p j d", j=4)).then_inc(sSC.h, 16)
                sSC.n += 16
                st["a2_store"] = sSC.n

        if qg < NQG:
            S_.wait_ge(sV.h, st["a2_z3"])
            if qg >= 2:
                S_.wait_ge(sX.h, st[f"a2_ptr{(qg - 2) % 3}"])
            hnd = S_.activation(pbuf[zt][:, :], zbuf[zt][:, :], AF.Exp,
                                bias=biascol_sb[:, qt:qt + 1], scale=1.0,
                                accum_out=denbuf[:, qt:qt + 1])
            n = sS.inc(hnd)
            st[f"a2_expz{zt}"] = n
            st[f"a2_expn{qg % 2}"] = n
            if qt == QT - 1:
                st["a2_expl"] = n
            X.wait_ge(sS.h, n)
            if qg >= 3:
                X.wait_ge(sT.h, st[f"a2_pv{(qg - 3) % 4}"])
            for pc in range(4):
                X.dma_start_transpose(
                    ptg[qg % 3].rearrange("p (pc m c) -> p pc m c",
                                          pc=4, m=KC // 4)[:, pc, :, :],
                    pbuf[zt].rearrange("p (pc m c) -> p pc m c",
                                       pc=4, m=KC // 4)[:, pc, :, :]).then_inc(sX.h, 16)
                sX.n += 16
            st[f"a2_ptr{qg % 3}"] = sX.n

    endA2 = dict(T=sT.n, V=sV.n, S=sS.n, X=sX.n, SC=sSC.n, DS=sDS.n)

    for s in range(nseq):
        for tt in range(S // 128):
            if s == 0 and tt == 0:
                S_.wait_ge(sSC.h, endA2["SC"])
            if st.get("a3_cons") is not None:
                S_.wait_ge(sV.h, st["a3_cons"])
            S_.dma_start(f32a[:, :], bass.AP(attsc, (s * S + tt * 128) * E,
                                             [[E, 128], [1, E]])).then_inc(sDS.h, 16)
            sDS.n += 16
            S_.dma_start(f32b[:, :], x_in[s, tt * 128:(tt + 1) * 128, :]).then_inc(sDS.h, 16)
            sDS.n += 16
            st["a3_r"] = sDS.n
            S_.wait_ge(sDS.h, st["a3_r"])
            if s == 0 and tt == 0:
                pre = endA2["V"]
            else:
                pre = None
            if st.get("a3_st") is not None:
                V.wait_ge(sSC.h, st["a3_st"])
            emit_ln(f32a[:, :], f32c[:, :], pre_wait_sv=pre)
            st["a3_ln"] = st["ln_out"]
            V.wait_ge(sV.h, st["ln_out"])
            st["a3_cons"] = sV.inc(
                V.tensor_tensor(f32c[:, :], f32c[:, :], f32b[:, :], ALU.add))
            S_.wait_ge(sV.h, st["a3_cons"])
            S_.dma_start(bass.AP(x1sc, (s * S + tt * 128) * E, [[E, 128], [1, E]]),
                         f32c[:, :]).then_inc(sSC.h, 16)
            sSC.n += 16
            st["a3_st"] = sSC.n

    endA3 = dict(T=sT.n, V=sV.n, S=sS.n, X=sX.n, SC=sSC.n, DS=sDS.n)

    def fslot_wait(i):
        if st.get(f"fsl{i}") is not None:
            T.wait_ge(sS.h, st[f"fsl{i}"])

    for ft in range(NFT):
        for sub in range(NSUB):
            if ft == 0 and sub == 0:
                S_.wait_ge(sSC.h, endA3["SC"])
                S_.wait_ge(sV.h, endA3["V"])
            if st.get("f1_n") is not None:
                S_.wait_ge(sV.h, st["f1_n"])
            off = (ft * TFF + sub * 128) * E
            S_.dma_start(f32a[:, :], bass.AP(x1sc, off, [[E, 128], [1, E]])).then_inc(sDS.h, 16)
            sDS.n += 16
            st["f1_r"] = sDS.n
            S_.wait_ge(sDS.h, st["f1_r"])
            emit_ln(f32a[:, :], xhat_tok[:, 0:E])
            st["f1_n"] = st["ln_out"]
            X.wait_ge(sV.h, st["f1_n"])
            if sub == 0:
                if ft == 0:
                    X.wait_ge(sT.h, endA3["T"])
                elif st.get("f5_done") is not None:
                    X.wait_ge(sT.h, st["f5_done"])
            X.dma_start_transpose(
                xh2_fm[:, 0:8 * TFF].rearrange("p (ic sb c) -> p sb ic c",
                                               ic=8, sb=NSUB)[:, sub, :, :],
                xhat_tok[:, 0:E].rearrange("p (ic c) -> p ic c", ic=8)).then_inc(sX.h, 16)
            sX.n += 16
            st["f1_tr"] = sX.n

        for hc in range(32):
            wb = hc % 2
            key = f"f2_wu{wb}"
            if st.get(key) is not None:
                G.wait_ge(sT.h, st[key])
            if hc < 2 and st.get(f"f3_wu{wb}") is not None:
                G.wait_ge(sT.h, st[f"f3_wu{wb}"])
            for wi, wt in ((0, w1T_d), (1, w2T_d)):
                G.dma_start(w12_sb[wb][:, wi * 1024:(wi + 1) * 1024],
                            bass.AP(wt, hc * 128 * 1024,
                                    [[1024, 128], [1, 1024]])).then_inc(sDW.h, 16)
                sDW.n += 16
            st[f"f2_w{wb}"] = sDW.n
            for hf in range(NHALF):
                g2 = hc * NHALF + hf
                sl0 = g2 % 2
                sl1 = 2 + g2 % 2
                T.wait_ge(sDW.h, st[f"f2_w{wb}"])
                T.wait_ge(sX.h, st["f1_tr"])
                if ft == 0 and hc == 0 and hf == 0:
                    T.wait_ge(sV.h, endA3["V"])
                fslot_wait(sl0)
                fslot_wait(sl1)
                for wi, sl in ((0, sl0), (1, sl1)):
                    for ic in range(8):
                        hnd = T.matmul(ps_a[:, sl * 512:(sl + 1) * 512],
                                       w12_sb[wb][:, wi * 1024 + ic * 128:
                                                  wi * 1024 + (ic + 1) * 128],
                                       xh2_fm[:, ic * TFF + hf * 512:
                                              ic * TFF + (hf + 1) * 512],
                                       start=(ic == 0), stop=(ic == 7))
                st["f2_mm"] = sT.inc(hnd)
                if hf == NHALF - 1:
                    st[f"f2_wu{wb}"] = sT.n
                S_.wait_ge(sT.h, st["f2_mm"])
                if st.get("f2_h") is not None:
                    S_.wait_ge(sV.h, st["f2_h"])
                S_.activation(ab_sb[:, 0:512], ps_a[:, sl0 * 512:(sl0 + 1) * 512],
                              AF.Silu, scale=s1_sb[:, hc:hc + 1])
                hnd = S_.activation(ab_sb[:, 512:1024], ps_a[:, sl1 * 512:(sl1 + 1) * 512],
                                    AF.Copy, scale=s2_sb[:, hc:hc + 1])
                st[f"fsl{sl0}"] = st[f"fsl{sl1}"] = st["f2_ev"] = sS.inc(hnd)
                V.wait_ge(sS.h, st["f2_ev"])
                st["f2_h"] = sV.inc(
                    V.tensor_tensor(h_buf[:, hc * TFF + hf * 512:hc * TFF + (hf + 1) * 512],
                                    ab_sb[:, 0:512], ab_sb[:, 512:1024], ALU.mult))

        g3 = 0
        for stage, (wt_d, scol, src_buf, dst_buf, nout, rowd) in enumerate((
                (h1T_d, h1s_sb, h_buf, g1_buf, 32, FH),
                (h2T_d, h2s_sb, g1_buf, h_buf, 32, FH),
                (w3T_d, w3s_sb, h_buf, w3fm, 8, E))):
            af = AF.Gelu if stage < 2 else AF.Copy
            for oc in range(nout):
                wb = oc % 2
                key = f"f3_wu{wb}"
                if st.get(key) is not None:
                    G.wait_ge(sT.h, st[key])
                if stage == 0 and oc < 2 and st.get(f"f2_wu{wb}") is not None:
                    G.wait_ge(sT.h, st[f"f2_wu{wb}"])
                G.dma_start(hst_sb[wb][:, :],
                            bass.AP(wt_d, oc * 128 * 4096,
                                    [[4096, 128], [1, 4096]])).then_inc(sDW.h, 16)
                sDW.n += 16
                st[f"f3_w{wb}"] = sDW.n
                for hf in range(NHALF):
                    sl = g3 % 4
                    g3 += 1
                    T.wait_ge(sDW.h, st[f"f3_w{wb}"])
                    if stage == 0:
                        T.wait_ge(sV.h, st["f2_h"])
                    elif stage == 1:
                        T.wait_ge(sS.h, st["f3_ev0"])
                    else:
                        T.wait_ge(sS.h, st["f3_ev1"])
                    fslot_wait(sl)
                    for ic in range(32):
                        hnd = T.matmul(ps_a[:, sl * 512:(sl + 1) * 512],
                                       hst_sb[wb][:, ic * 128:(ic + 1) * 128],
                                       src_buf[:, ic * TFF + hf * 512:
                                               ic * TFF + (hf + 1) * 512],
                                       start=(ic == 0), stop=(ic == 31))
                    st["f3_mm"] = sT.inc(hnd)
                    if hf == NHALF - 1:
                        st[f"f3_wu{wb}"] = sT.n
                    if stage == 2 and oc == nout - 1 and hf == NHALF - 1:
                        st["f5_done"] = sT.n
                    S_.wait_ge(sT.h, st["f3_mm"])
                    if stage == 2 and oc == 0 and hf == 0 and st.get("f6_tr") is not None:
                        S_.wait_ge(sX.h, st["f6_tr"])
                    if stage == 2:
                        dst_ap = w3fm.rearrange("p (sb ic c) -> p ic sb c",
                                                sb=NSUB, ic=8)[:, oc, 4 * hf:4 * hf + 4, :]
                        hnd = S_.activation(dst_ap,
                                            ps_a[:, sl * 512:(sl + 1) * 512]
                                            .rearrange("p (sb c) -> p sb c", sb=4),
                                            af, scale=scol[:, oc:oc + 1])
                    else:
                        hnd = S_.activation(dst_buf[:, oc * TFF + hf * 512:
                                                    oc * TFF + (hf + 1) * 512],
                                            ps_a[:, sl * 512:(sl + 1) * 512],
                                            af, scale=scol[:, oc:oc + 1])
                    st[f"fsl{sl}"] = st[f"f3_ev{stage}"] = sS.inc(hnd)

        for sub in range(NSUB):
            X.wait_ge(sS.h, st["f3_ev2"])
            if st.get("f6_cp") is not None:
                X.wait_ge(sV.h, st["f6_cp"])
            X.dma_start_transpose(
                xhat_fm[:, 0:E].rearrange("p (ic c) -> p ic c", ic=8),
                w3fm[:, sub * E:(sub + 1) * E]
                .rearrange("p (ic c) -> p ic c", ic=8)).then_inc(sX.h, 16)
            sX.n += 16
            st["f6_tr"] = sX.n
            if st.get("f6_add") is not None:
                S_.wait_ge(sV.h, st["f6_add"])
            off = (ft * TFF + sub * 128) * E
            S_.dma_start(f32c[:, :], bass.AP(x1sc, off, [[E, 128], [1, E]])).then_inc(sDS.h, 16)
            sDS.n += 16
            st["f6_r"] = sDS.n
            V.wait_ge(sX.h, st["f6_tr"])
            if st.get("f6_st") is not None:
                V.wait_ge(sOUT.h, st["f6_st"])
            st["f6_cp"] = sV.inc(V.tensor_copy(f32b[:, :], xhat_fm[:, 0:E]))
            emit_ln(f32b[:, :], f32b[:, :], pre_wait_sv=st["f6_cp"])
            V.wait_ge(sV.h, st["ln_out"])
            V.wait_ge(sDS.h, st["f6_r"])
            st["f6_add"] = sV.inc(
                V.tensor_tensor(f32b[:, :], f32b[:, :], f32c[:, :], ALU.add))
            S_.wait_ge(sV.h, st["f6_add"])
            tok = ft * TFF + sub * 128
            S_.dma_start(out_d[tok // S, tok % S:tok % S + 128, :],
                         f32b[:, :]).then_inc(sOUT.h, 16)
            sOUT.n += 16
            st["f6_st"] = sOUT.n

    S_.wait_ge(sOUT.h, sOUT.n)

    def replay(eng, rec):
        for ent in rec.calls:
            hnd = getattr(eng, ent["op"])(*ent["a"], **ent["k"])
            for (semh, val) in ent["chain"]:
                hnd = hnd.then_inc(semh, val)

    with nc.Block() as block:
        @block.tensor
        def _(t):
            replay(t, T)

        @block.vector
        def _(v):
            replay(v, V)

        @block.scalar
        def _(s2):
            replay(s2, S_)

        @block.sync
        def _(x2):
            replay(x2, X)

        @block.gpsimd
        def _(g2):
            replay(g2, G)

    for cm in cms[::-1]:
        cm.__exit__(None, None, None)
    for sm in (sT, sV, sS, sG, sX, sDW, sDS, sSC, sOUT):
        sm._cm.__exit__(None, None, None)
    return nc


def prep_inputs(L0_attn_x, L0_attn_Wq, L0_attn_bq, L0_attn_Wv, L0_attn_bv,
                L0_attn_hyperplanes, L0_ffn_w1_w, L0_ffn_w1_s, L0_ffn_w2_w, L0_ffn_w2_s,
                L0_ffn_h1_w, L0_ffn_h1_s, L0_ffn_h2_w, L0_ffn_h2_s, L0_ffn_w3_w, L0_ffn_w3_s):
    def f(a):
        return np.ascontiguousarray(np.asarray(a, dtype=np.float32))
    def tile_in(wT):
        K_, M_ = wT.shape
        return np.ascontiguousarray(
            wT.reshape(K_ // 128, 128, M_ // 128, 128).transpose(2, 1, 0, 3))

    def tile_qv(wT):
        return np.ascontiguousarray(wT.reshape(8, 128, E).transpose(1, 0, 2))

    com = dict(
        wqT=f(tile_qv(np.asarray(L0_attn_Wq, dtype=np.float32).T)),
        wvT=f(tile_qv(np.asarray(L0_attn_Wv, dtype=np.float32).T)),
        bq=f(L0_attn_bq), bv=f(L0_attn_bv), hyp=f(L0_attn_hyperplanes),
        w1T=f(tile_in(np.asarray(L0_ffn_w1_w, dtype=np.float32).T)),
        w2T=f(tile_in(np.asarray(L0_ffn_w2_w, dtype=np.float32).T)),
        h1T=f(tile_in(np.asarray(L0_ffn_h1_w, dtype=np.float32).T)),
        h2T=f(tile_in(np.asarray(L0_ffn_h2_w, dtype=np.float32).T)),
        w3T=f(tile_in(np.asarray(L0_ffn_w3_w, dtype=np.float32).T)),
        s1=f(L0_ffn_w1_s), s2=f(L0_ffn_w2_s), h1s=f(L0_ffn_h1_s), h2s=f(L0_ffn_h2_s),
        w3s=f(L0_ffn_w3_s))
    x = f(L0_attn_x)
    return x, com


def kernel(**inputs):
    x, com = prep_inputs(**inputs)
    B, S, _ = x.shape
    ncore = 8
    nseq = B // ncore
    nc = build_program(nseq, S)
    in_maps = []
    for c in range(ncore):
        m = dict(com)
        m["x"] = np.ascontiguousarray(x[c * nseq:(c + 1) * nseq])
        in_maps.append(m)
    res = run_bass_kernel_spmd(nc, in_maps, core_ids=list(range(ncore)))
    out = np.concatenate([np.asarray(r["out"]) for r in res.results], axis=0)
    return out.reshape(B, S, E)


# version 09
# speedup vs baseline: 1.4147x; 1.4147x over previous
import numpy as np
import concourse.bass as bass
import concourse.mybir as mybir
from concourse.bass_utils import run_bass_kernel_spmd

F32 = mybir.dt.float32
BF16 = mybir.dt.bfloat16
AF = mybir.ActivationFunctionType
ALU = mybir.AluOpType
AX = mybir.AxisListType.X

E = 1024
H = 16
DH = 64
NH = 6
FH = 4096
ISQ32 = float(1.0 / np.sqrt(np.float32(32.0)))
EPS = 1e-5


class Rec:

    def __init__(self, name):
        self.name = name
        self.calls = []

    def __getattr__(self, op):
        if op.startswith("_"):
            raise AttributeError(op)
        calls = self.calls

        def call(*a, **k):
            ent = {"op": op, "a": a, "k": k, "chain": []}
            calls.append(ent)

            class Hnd:
                def then_inc(self, sem, val=1):
                    ent["chain"].append((sem, val))
                    return self
            return Hnd()
        return call


class Sem:
    def __init__(self, nc, name):
        self._cm = nc.semaphore(name)
        self.h = self._cm.__enter__()
        self.n = 0

    def inc(self, _hnd, k=1):
        _hnd.then_inc(self.h, k)
        self.n += k
        return self.n


def build_program(nseq, S, debug=False):
    nc = bass.Bass("TRN2", target_bir_lowering=False)
    NT = nseq * S
    QT = S // 128
    KC = S // 128
    GR = S // 256
    NHEAD = nseq * H
    NQG = NHEAD * QT
    TFF = min(1024, NT)
    NFT = NT // TFF
    NSUB = TFF // 128
    NHALF = TFF // 512

    x_in = nc.dram_tensor("x", [nseq, S, E], F32, kind="ExternalInput")
    wqT_d = nc.dram_tensor("wqT", [128, 8, E], F32, kind="ExternalInput")
    wvT_d = nc.dram_tensor("wvT", [128, 8, E], F32, kind="ExternalInput")
    bq_d = nc.dram_tensor("bq", [E], F32, kind="ExternalInput")
    bv_d = nc.dram_tensor("bv", [E], F32, kind="ExternalInput")
    hyp_d = nc.dram_tensor("hyp", [65, NH], F32, kind="ExternalInput")
    w1T_d = nc.dram_tensor("w1T", [32, 128, 8, 128], F32, kind="ExternalInput")
    w2T_d = nc.dram_tensor("w2T", [32, 128, 8, 128], F32, kind="ExternalInput")
    h1T_d = nc.dram_tensor("h1T", [32, 128, 32, 128], F32, kind="ExternalInput")
    h2T_d = nc.dram_tensor("h2T", [32, 128, 32, 128], F32, kind="ExternalInput")
    w3T_d = nc.dram_tensor("w3T", [8, 128, 32, 128], F32, kind="ExternalInput")
    s1_d = nc.dram_tensor("s1", [FH], F32, kind="ExternalInput")
    s2_d = nc.dram_tensor("s2", [FH], F32, kind="ExternalInput")
    h1s_d = nc.dram_tensor("h1s", [FH], F32, kind="ExternalInput")
    h2s_d = nc.dram_tensor("h2s", [FH], F32, kind="ExternalInput")
    w3s_d = nc.dram_tensor("w3s", [E], F32, kind="ExternalInput")
    out_d = nc.dram_tensor("out", [nseq, S, E], F32, kind="ExternalOutput")
    qsc = nc.dram_tensor("qsc", [nseq, 8, 128, S], BF16)
    vsc = nc.dram_tensor("vsc", [nseq, H, 128, KC, DH], BF16)
    attsc = nc.dram_tensor("attsc", [nseq, S, E], F32)
    x1sc = nc.dram_tensor("x1sc", [nseq, S, E], F32)

    cms = []

    def sbuf(name, shape, dt):
        cm = nc.sbuf_tensor(name, shape, dt)
        t = cm.__enter__()
        cms.append(cm)
        return t

    def psum(name, shape, dt=F32):
        cm = nc.psum_tensor(name, shape, dt)
        t = cm.__enter__()
        cms.append(cm)
        return t

    ptg3 = sbuf("ptg3", [128, 4096], BF16)
    iota_p = sbuf("iota_p", [128, 1], F32)
    hyp_sb = sbuf("hyp_sb", [64, NH], BF16)
    hb_col = sbuf("hb_col", [NH, 1], F32)
    pw2 = sbuf("pw2", [NH, 1], BF16)
    ones_col = sbuf("ones_col", [128, 1], BF16)
    neg63_col = sbuf("neg63_col", [128, 1], BF16)
    ones_row = sbuf("ones_row", [1, 128], BF16)
    bq_sb = sbuf("bq_sb", [128, 8], F32)
    bv_sb = sbuf("bv_sb", [128, 8], F32)
    s1_sb = sbuf("s1_sb", [128, 32], F32)
    s2_sb = sbuf("s2_sb", [128, 32], F32)
    h1s_sb = sbuf("h1s_sb", [128, 32], F32)
    h2s_sb = sbuf("h2s_sb", [128, 32], F32)
    w3s_sb = sbuf("w3s_sb", [128, 8], F32)
    lncol = sbuf("lncol", [128, 8], F32)
    biascol_sb = sbuf("biascol_sb", [128, max(QT, 1)], F32)
    denbuf = sbuf("denbuf", [128, max(QT, 4)], F32)
    denr = sbuf("denr", [128, max(QT, 4)], F32)
    bits_sb = sbuf("bits_sb", [NH, 512], BF16)
    bucket_sb = sbuf("bucket_sb", [1, 512], BF16)
    bqall = sbuf("bqall", [128, max(QT, 1)], F32)

    big0 = sbuf("big0", [128, 32768], BF16)
    big1 = sbuf("big1", [128, 32768], BF16)
    wq_sb = sbuf("wq_sb", [128, 8192], BF16)
    wv_sb = sbuf("wv_sb", [128, 8192], BF16)
    hst_sb = [sbuf("hsta", [128, 4096], BF16), sbuf("hstb", [128, 4096], BF16)]
    w12_sb = [hst_sb[0][:, 0:2048], hst_sb[1][:, 0:2048]]
    ab_sb = sbuf("ab_sb", [128, 1024], BF16)
    f32ar = sbuf("f32ar", [128, 4096], F32)

    ptg = [big0[:, 0:4096], big0[:, 4096:8192], ptg3[:, :]]
    qbuf = [big0[:, 8192:12288], big0[:, 12288:16384]]
    zbuf = [big0[:, 16384:20480], big0[:, 20480:24576]]
    pbuf = [big0[:, 24576:28672], big0[:, 28672:32768]]
    sq_sb = big1[:, 0:4096]
    bcb_sb = big1[:, 4096:8192]
    ctile = big1[:, 8192:12288]
    vbuf = [big1[:, 12288:14336], big1[:, 14336:16384]]
    xhat_tok = big1[:, 16384:18432]
    xhat_fm = big1[:, 18432:20480]
    vfm_grp = big1[:, 20480:22528]
    vtok_grp = big1[:, 22528:24576]
    qev = [big1[:, 24576:24832], big1[:, 24832:25088]]
    h_buf = big0
    g1_buf = big1
    xh2_fm = wq_sb
    w3fm = wv_sb
    xbuf = f32ar[:, 0:2048]
    xsq = f32ar[:, 2048:3072]
    attsl = f32ar[:, 3584:3840]
    f32a = f32ar[:, 0:1024]
    f32b = f32ar[:, 1024:2048]
    f32c = f32ar[:, 3072:4096]

    ps_a = psum("ps_a", [128, 2048])
    ps_b = psum("ps_b", [128, 1024])
    ps_c = psum("ps_c", [128, 512])
    ps_d = psum("ps_d", [128, 512])

    sT = Sem(nc, "sT")
    sV = Sem(nc, "sV")
    sS = Sem(nc, "sS")
    sG = Sem(nc, "sG")
    sX = Sem(nc, "sX")
    sDW = Sem(nc, "sDW")
    sDS = Sem(nc, "sDS")
    sSC = Sem(nc, "sSC")
    sOUT = Sem(nc, "sOUT")

    T, V, S_, X, G = Rec("T"), Rec("V"), Rec("S"), Rec("X"), Rec("G")

    G.iota(iota_p[:, :], [[0, 1]], channel_multiplier=1,
           allow_small_or_imprecise_dtypes=True)
    G.memset(ones_col[:, :], 1.0)
    G.memset(neg63_col[:, :], -63.0)
    g_init = sG.inc(G.memset(ones_row[:, :], 1.0))

    G.dma_start(wq_sb[:, :], bass.AP(wqT_d, 0, [[8 * E, 128], [1, 8 * E]])).then_inc(sDW.h, 16)
    sDW.n += 16
    G.dma_start(wv_sb[:, :], bass.AP(wvT_d, 0, [[8 * E, 128], [1, 8 * E]])).then_inc(sDW.h, 16)
    sDW.n += 16
    w_qv = sDW.n
    for t_, src_ in ((hyp_sb, hyp_d[0:64, :]),
                     (hb_col, hyp_d[64:65, :].rearrange("a n -> n a")),
                     (bq_sb, bq_d[:].rearrange("(c p) -> p c", p=128)),
                     (bv_sb, bv_d[:].rearrange("(c p) -> p c", p=128)),
                     (s1_sb, s1_d[:].rearrange("(c p) -> p c", p=128)),
                     (s2_sb, s2_d[:].rearrange("(c p) -> p c", p=128)),
                     (h1s_sb, h1s_d[:].rearrange("(c p) -> p c", p=128)),
                     (h2s_sb, h2s_d[:].rearrange("(c p) -> p c", p=128)),
                     (w3s_sb, w3s_d[:].rearrange("(c p) -> p c", p=128))):
        G.dma_start(t_[:, :], src_, allow_slow_non_contiguous=True).then_inc(sDW.h, 16)
        sDW.n += 16
    w_misc = sDW.n

    V.wait_ge(sG.h, g_init)
    V.wait_ge(sDW.h, w_misc)
    V.tensor_scalar(bq_sb[:, :], bq_sb[:, :], ISQ32, None, ALU.mult)
    v_init = sV.inc(V.tensor_scalar(hb_col[:, :], hb_col[:, :], ISQ32, None, ALU.mult))
    T.wait_ge(sV.h, v_init)
    S_.wait_ge(sV.h, v_init)
    S_.wait_ge(sG.h, g_init)
    sS.inc(S_.activation(pw2[0:NH, :], iota_p[0:NH, :], AF.Exp,
                         scale=float(np.log(2.0))))

    st = {}

    def vstep(hnd):
        n = sV.inc(hnd)
        V.wait_ge(sV.h, n)
        return n

    def emit_ln(src_ap, dst_ap, pre_wait_sv=None):
        if pre_wait_sv is not None:
            S_.wait_ge(sV.h, pre_wait_sv)
        if st.get("ln_rd") is not None:
            S_.wait_ge(sV.h, st["ln_rd"])
        st["ln_sq"] = sS.inc(S_.activation(xsq[:, :], src_ap, AF.Square))
        V.wait_ge(sS.h, st["ln_sq"])
        V.reduce_sum(lncol[:, 0:1], src_ap, axis=AX)
        vstep(V.reduce_sum(lncol[:, 1:2], xsq[:, :], axis=AX))
        vstep(V.tensor_scalar(lncol[:, 2:3], lncol[:, 0:1], 1.0 / E, None, ALU.mult))
        vstep(V.tensor_tensor(lncol[:, 3:4], lncol[:, 2:3], lncol[:, 2:3], ALU.mult))
        vstep(V.tensor_scalar(lncol[:, 4:5], lncol[:, 1:2], 1.0 / E, lncol[:, 3:4],
                              ALU.mult, ALU.subtract))
        st["ln_var"] = sV.inc(
            V.tensor_scalar(lncol[:, 4:5], lncol[:, 4:5], EPS, None, ALU.add))
        st["ln_rd"] = st["ln_var"]
        S_.wait_ge(sV.h, st["ln_var"])
        st["ln_sd"] = sS.inc(S_.activation(lncol[:, 5:6], lncol[:, 4:5], AF.Sqrt))
        V.wait_ge(sS.h, st["ln_sd"])
        vstep(V.reciprocal(lncol[:, 6:7], lncol[:, 5:6]))
        st["ln_out"] = sV.inc(
            V.tensor_scalar(dst_ap, src_ap, lncol[:, 2:3], lncol[:, 6:7],
                            ALU.subtract, ALU.mult))
        return st["ln_out"]

    for s in range(nseq):
        for g in range(GR):
            t0 = g * 256
            if st.get("a1_xcons") is not None:
                S_.wait_ge(sV.h, st["a1_xcons"])
            src = x_in[s, t0:t0 + 256, :].rearrange("(j p) f -> p j f", p=128)
            S_.dma_start(xbuf[:, :].rearrange("p (j f) -> p j f", j=2), src).then_inc(sDS.h, 16)
            sDS.n += 16
            st["a1_xr"] = sDS.n
            for j in range(2):
                S_.wait_ge(sDS.h, st["a1_xr"])
                ln_out = emit_ln(xbuf[:, j * E:(j + 1) * E],
                                 xhat_tok[:, j * E:(j + 1) * E])
                if j == 1:
                    st["a1_xcons"] = ln_out
            X.wait_ge(sV.h, st["ln_out"])
            if st.get("a1_projdone") is not None:
                X.wait_ge(sT.h, st["a1_projdone"])
            for j in range(2):
                X.dma_start_transpose(
                    xhat_fm.rearrange("p (ic j c) -> p ic j c", ic=8, j=2)[:, :, j, :],
                    xhat_tok[:, j * E:(j + 1) * E]
                    .rearrange("p (ic c) -> p ic c", ic=8)).then_inc(sX.h, 16)
                sX.n += 16
            st["a1_xfm"] = sX.n

            for wi, (w_sb, bias_sb, scl) in enumerate(
                    ((wq_sb, bq_sb, ISQ32), (wv_sb, bv_sb, 1.0))):
                for oc in range(8):
                    slot_off = wi * 1024 + (oc % 2) * 512
                    pst = ps_a[:, slot_off:slot_off + 256]
                    T.wait_ge(sX.h, st["a1_xfm"])
                    if wi == 0 and oc == 0:
                        T.wait_ge(sDW.h, w_qv)
                    key = f"a1_sl{wi}{oc % 2}"
                    if st.get(key) is not None:
                        T.wait_ge(sV.h, st[key])
                    for ic in range(8):
                        hnd = T.matmul(pst,
                                       w_sb[:, ic * E + oc * 128: ic * E + (oc + 1) * 128],
                                       xhat_fm[:, ic * 256:(ic + 1) * 256],
                                       start=(ic == 0), stop=(ic == 7))
                    st[f"a1_mm{wi}"] = sT.inc(hnd)
                    if wi == 1 and oc == 7:
                        st["a1_projdone"] = sT.n
                    V.wait_ge(sT.h, st[f"a1_mm{wi}"])
                    if wi == 0:
                        qv = qev[oc % 2]
                        if st.get(f"a1_qst{oc % 2}") is not None:
                            V.wait_ge(sSC.h, st[f"a1_qst{oc % 2}"])
                        st[key] = sV.inc(
                            V.tensor_scalar(qv[:, :], pst,
                                            bias_sb[:, oc:oc + 1], scl,
                                            ALU.add, ALU.mult))
                        S_.wait_ge(sV.h, st[key])
                        S_.dma_start(qsc[s, oc, :, t0:t0 + 256], qv[:, :]).then_inc(sSC.h, 16)
                        sSC.n += 16
                        st[f"a1_qst{oc % 2}"] = sSC.n
                    else:
                        if st.get("a1_vtr") is not None and oc == 0:
                            V.wait_ge(sX.h, st["a1_vtr"])
                        st[key] = sV.inc(
                            V.tensor_scalar(
                                vfm_grp.rearrange("p (tj oc c) -> p oc tj c",
                                                  tj=2, oc=8)[:, oc, :, :],
                                pst.rearrange("p (tj c) -> p tj c", tj=2),
                                bias_sb[:, oc:oc + 1], None, ALU.add))
                        st["a1_ev1"] = st[key]
            X.wait_ge(sV.h, st["a1_ev1"])
            if st.get("a1_vst") is not None:
                X.wait_ge(sDS.h, st["a1_vst"])
            for tj in range(2):
                X.dma_start_transpose(
                    vtok_grp[:, tj * 1024:(tj + 1) * 1024]
                    .rearrange("p (oc c) -> p oc c", oc=8),
                    vfm_grp[:, tj * 1024:(tj + 1) * 1024]
                    .rearrange("p (oc c) -> p oc c", oc=8)).then_inc(sX.h, 16)
                sX.n += 16
            st["a1_vtr"] = sX.n
            S_.wait_ge(sX.h, st["a1_vtr"])
            for tj in range(2):
                dst = bass.AP(vsc, s * H * 128 * KC * DH + (g * 2 + tj) * DH,
                              [[KC * DH, 128], [128 * KC * DH, 16], [1, DH]])
                S_.dma_start(dst, vtok_grp[:, tj * 1024:(tj + 1) * 1024]
                             .rearrange("p (hh d) -> p hh d", hh=16)).then_inc(sDS.h, 16)
                sDS.n += 16
            st["a1_vst"] = sDS.n

    endA1 = dict(T=sT.n, V=sV.n, S=sS.n, X=sX.n, SC=sSC.n, DS=sDS.n)

    for qg in range(NQG + 8):
        def emit_pv(qp):
            hip = qp // QT
            bufp = hip % 2
            if st.get(f"a2_evac{qp % 4}") is not None:
                T.wait_ge(sS.h, st[f"a2_evac{qp % 4}"])
            for m in range(KC):
                if m % (KC // 4) == 0:
                    T.wait_ge(sX.h, st[f"a2_ptrp{qp % 3}_{m // (KC // 4)}"])
                hnd = T.matmul(ps_d[:, (qp % 4) * 64:(qp % 4) * 64 + 64],
                               ptg[qp % 3][:, m * 128:(m + 1) * 128],
                               vbuf[bufp][:, m * DH:(m + 1) * DH],
                               start=(m == 0), stop=(m == KC - 1))
            st[f"a2_pv{qp % 4}"] = sT.inc(hnd)
            if qp % QT == QT - 1:
                st[f"a2_vfree{bufp}"] = sT.n

        if qg >= NQG and qg - 1 < NQG:
            emit_pv(qg - 1)

        if qg < NQG:
            hi = qg // QT
            qt = qg % QT
            s, h = hi // H, hi % H
            buf = hi % 2
            zt = qg % 2
            if qt == 0:
                if hi == 0:
                    G.wait_ge(sSC.h, endA1["SC"])
                    G.wait_ge(sDS.h, endA1["DS"])
                if st.get(f"a2_qfree{buf}") is not None:
                    G.wait_ge(sT.h, st[f"a2_qfree{buf}"])
                if st.get(f"a2_vfree{buf}") is not None:
                    G.wait_ge(sT.h, st[f"a2_vfree{buf}"])
                G.dma_start(qbuf[buf][0:64, :],
                            qsc[s, h // 2, 64 * (h % 2):64 * (h % 2) + 64, :]).then_inc(sDS.h, 16)
                sDS.n += 16
                G.dma_start(vbuf[buf],
                            bass.AP(vsc, (s * H + h) * 128 * KC * DH,
                                    [[KC * DH, 128], [1, KC * DH]])).then_inc(sDS.h, 16)
                sDS.n += 16
                st[f"a2_qv{buf}"] = sDS.n
                S_.wait_ge(sDS.h, st[f"a2_qv{buf}"])
                if st.get("a2_sqfree") is not None:
                    S_.wait_ge(sT.h, st["a2_sqfree"])
                st["a2_sq"] = sS.inc(S_.activation(sq_sb[0:64, :], qbuf[buf][0:64, :],
                                                   AF.Square))
                T.wait_ge(sS.h, st["a2_sq"])
                if st.get("a2_bcp") is not None:
                    T.wait_ge(sV.h, st["a2_bcp"])
                for qq in range(QT):
                    hnd = T.matmul(ps_d[:, 256 + qq:257 + qq],
                                   sq_sb[0:64, qq * 128:(qq + 1) * 128],
                                   neg63_col[0:64, :], start=True, stop=True)
                st["a2_s2b"] = sT.inc(hnd)
                V.wait_ge(sT.h, st["a2_s2b"])
                if st.get("a2_expl") is not None:
                    V.wait_ge(sS.h, st["a2_expl"])
                st["a2_bcp"] = sV.inc(V.tensor_copy(biascol_sb[:, :],
                                                    ps_d[:, 256:256 + QT]))
                for ch in range(S // 512):
                    T.wait_ge(sDS.h, st[f"a2_qv{buf}"])
                    if hi == 0 and ch == 0:
                        T.wait_ge(sS.h, endA1["S"])
                        T.wait_ge(sV.h, endA1["V"])
                    if ch == 0 and st.get("a2_z3") is not None:
                        T.wait_ge(sV.h, st["a2_z3"])
                    if st.get("a2_bits") is not None:
                        T.wait_ge(sV.h, st["a2_bits"])
                    hnd = T.matmul(ps_b[0:NH, 0:512], hyp_sb[:, :],
                                   qbuf[buf][0:64, ch * 512:(ch + 1) * 512],
                                   start=True, stop=True)
                    st["a2_bp"] = sT.inc(hnd)
                    V.wait_ge(sT.h, st["a2_bp"])
                    if ch == 0 and st.get("a2_ctG") is not None:
                        V.wait_ge(sG.h, st["a2_ctG"])
                    if st.get("a2_sum") is not None:
                        V.wait_ge(sT.h, st["a2_sum"])
                    st["a2_bits"] = sV.inc(
                        V.tensor_scalar(bits_sb[:, :], ps_b[0:NH, 0:512],
                                        hb_col[:, 0:1], 0.0, ALU.add, ALU.is_ge))
                    T.wait_ge(sV.h, st["a2_bits"])
                    if st.get("a2_bcbev") is not None:
                        T.wait_ge(sV.h, st["a2_bcbev"])
                    hnd = T.matmul(ps_c[0:1, 0:512], pw2[:, :],
                                   bits_sb[:, :], start=True, stop=True)
                    if st.get("a2_bqc") is not None:
                        T.wait_ge(sV.h, st["a2_bqc"])
                    for i4 in range(4):
                        hnd = T.matmul(ps_d[:, 288 + i4:289 + i4],
                                       bits_sb[:, i4 * 128:(i4 + 1) * 128], pw2[:, :],
                                       start=True, stop=True)
                    st["a2_sum"] = sT.inc(hnd)
                    V.wait_ge(sT.h, st["a2_sum"])
                    V.tensor_copy(bucket_sb[0:1, :], ps_c[0:1, 0:512])
                    st["a2_bqc"] = sV.inc(
                        V.tensor_copy(bqall[:, ch * 4:(ch + 1) * 4], ps_d[:, 288:292]))
                    T.wait_ge(sV.h, st["a2_bqc"])
                    hnd = T.matmul(ps_c[0:128, 0:512], ones_row[0:1, :],
                                   bucket_sb[0:1, :], start=True, stop=True)
                    st["a2_bc"] = sT.inc(hnd)
                    V.wait_ge(sT.h, st["a2_bc"])
                    st["a2_bcbev"] = sV.inc(
                        V.tensor_copy(bcb_sb[:, ch * 512:(ch + 1) * 512],
                                      ps_c[0:128, 0:512]))
                st["a2_bcb"] = st["a2_bcbev"]

            V.wait_ge(sV.h, st["a2_bcb"])
            st["a2_ct"] = ("V", sV.inc(
                V.tensor_scalar(ctile[:, :], bcb_sb[:, :], bqall[:, qt:qt + 1],
                                62.0, ALU.is_equal, ALU.add)))

            if st.get("a2_z1") is not None:
                T.wait_ge(sV.h, st["a2_z1"])
            for kk in range(4):
                hnd = T.matmul(ps_a[:, kk * 512:(kk + 1) * 512],
                               qbuf[buf][0:64, qt * 128:(qt + 1) * 128],
                               qbuf[buf][0:64, kk * 512:(kk + 1) * 512],
                               start=True, stop=True)
            st["a2_qkA"] = sT.inc(hnd)
            if qt == 0:
                T.wait_ge(sV.h, st["a2_bits"])
            if st.get("a2_z2") is not None:
                T.wait_ge(sV.h, st["a2_z2"])
            for kk in range(2):
                hnd = T.matmul(ps_b[:, kk * 512:(kk + 1) * 512],
                               qbuf[buf][0:64, qt * 128:(qt + 1) * 128],
                               qbuf[buf][0:64, 2048 + kk * 512:2048 + (kk + 1) * 512],
                               start=True, stop=True)
            st["a2_qkB"] = sT.inc(hnd)

            if qg - 1 >= 0:
                emit_pv(qg - 1)

            ctl, ctn = st["a2_ct"]
            if ctl == "G":
                V.wait_ge(sG.h, ctn)
            else:
                V.wait_ge(sV.h, ctn)
            V.wait_ge(sT.h, st["a2_qkA"])
            if st.get(f"a2_expz{zt}") is not None:
                V.wait_ge(sS.h, st[f"a2_expz{zt}"])
            st["a2_z1"] = sV.inc(
                V.tensor_tensor(zbuf[zt][:, 0:2048], ctile[:, 0:2048],
                                ps_a[:, 0:2048], ALU.mult))
            V.wait_ge(sT.h, st["a2_qkB"])
            st["a2_z2"] = sV.inc(
                V.tensor_tensor(zbuf[zt][:, 2048:3072], ctile[:, 2048:3072],
                                ps_b[:, 0:1024], ALU.mult))

            T.wait_ge(sV.h, st["a2_z2"])
            for kk in range(2):
                hnd = T.matmul(ps_b[:, kk * 512:(kk + 1) * 512],
                               qbuf[buf][0:64, qt * 128:(qt + 1) * 128],
                               qbuf[buf][0:64, 3072 + kk * 512:3072 + (kk + 1) * 512],
                               start=True, stop=True)
            st["a2_qkB2"] = sT.inc(hnd)
            if qt == QT - 1:
                st[f"a2_qfree{buf}"] = sT.n
                st["a2_sqfree"] = sT.n

            V.wait_ge(sT.h, st["a2_qkB2"])
            st["a2_z3"] = sV.inc(
                V.tensor_tensor(zbuf[zt][:, 3072:4096], ctile[:, 3072:4096],
                                ps_b[:, 0:1024], ALU.mult))

        if qg % 4 == 0 and 0 < qg <= NQG:
            lo = qg - 4
            V.wait_ge(sS.h, st[f"a2_expn{(qg - 1) % 2}"])
            st["a2_den"] = sV.inc(
                V.reciprocal(denr[:, lo % QT:lo % QT + 4],
                             denbuf[:, lo % QT:lo % QT + 4]))

        if 0 <= qg - 4 < NQG:
            qe = qg - 4
            S_.wait_ge(sT.h, st[f"a2_pv{qe % 4}"])
            S_.wait_ge(sV.h, st["a2_den"])
            if qe % 4 == 0 and st.get("a2_store") is not None:
                S_.wait_ge(sSC.h, st["a2_store"])
            hnd = S_.activation(attsl[:, (qe % 4) * 64:(qe % 4) * 64 + 64],
                                ps_d[:, (qe % 4) * 64:(qe % 4) * 64 + 64],
                                AF.Copy, scale=denr[:, qe % QT:qe % QT + 1])
            st[f"a2_evac{qe % 4}"] = sS.inc(hnd)
            if qe % 4 == 3:
                G.wait_ge(sS.h, st[f"a2_evac{qe % 4}"])
                hie = qe // QT
                se, he = hie // H, hie % H
                qs = (qe % QT) // 4
                dst = bass.AP(attsc, (se * S + qs * 512) * E + he * DH,
                              [[E, 128], [128 * E, 4], [1, DH]])
                G.dma_start(dst, attsl[:, :]
                            .rearrange("p (j d) -> p j d", j=4)).then_inc(sSC.h, 16)
                sSC.n += 16
                st["a2_store"] = sSC.n

        if qg < NQG:
            S_.wait_ge(sV.h, st["a2_z3"])
            if qg >= 2:
                S_.wait_ge(sX.h, st[f"a2_ptr{(qg - 2) % 3}"])
            hnd = S_.activation(pbuf[zt][:, :], zbuf[zt][:, :], AF.Exp,
                                bias=biascol_sb[:, qt:qt + 1], scale=1.0,
                                accum_out=denbuf[:, qt:qt + 1])
            n = sS.inc(hnd)
            st[f"a2_expz{zt}"] = n
            st[f"a2_expn{qg % 2}"] = n
            if qt == QT - 1:
                st["a2_expl"] = n
            X.wait_ge(sS.h, n)
            if qg >= 3:
                X.wait_ge(sT.h, st[f"a2_pv{(qg - 3) % 4}"])
            for pc in range(4):
                X.dma_start_transpose(
                    ptg[qg % 3].rearrange("p (pc m c) -> p pc m c",
                                          pc=4, m=KC // 4)[:, pc, :, :],
                    pbuf[zt].rearrange("p (pc m c) -> p pc m c",
                                       pc=4, m=KC // 4)[:, pc, :, :]).then_inc(sX.h, 16)
                sX.n += 16
                st[f"a2_ptrp{qg % 3}_{pc}"] = sX.n
            st[f"a2_ptr{qg % 3}"] = sX.n

    endA2 = dict(T=sT.n, V=sV.n, S=sS.n, X=sX.n, SC=sSC.n, DS=sDS.n)

    for s in range(nseq):
        for tt in range(S // 128):
            if s == 0 and tt == 0:
                S_.wait_ge(sSC.h, endA2["SC"])
            if st.get("a3_cons") is not None:
                S_.wait_ge(sV.h, st["a3_cons"])
            S_.dma_start(f32a[:, :], bass.AP(attsc, (s * S + tt * 128) * E,
                                             [[E, 128], [1, E]])).then_inc(sDS.h, 16)
            sDS.n += 16
            S_.dma_start(f32b[:, :], x_in[s, tt * 128:(tt + 1) * 128, :]).then_inc(sDS.h, 16)
            sDS.n += 16
            st["a3_r"] = sDS.n
            S_.wait_ge(sDS.h, st["a3_r"])
            if s == 0 and tt == 0:
                pre = endA2["V"]
            else:
                pre = None
            if st.get("a3_st") is not None:
                V.wait_ge(sSC.h, st["a3_st"])
            emit_ln(f32a[:, :], f32c[:, :], pre_wait_sv=pre)
            st["a3_ln"] = st["ln_out"]
            V.wait_ge(sV.h, st["ln_out"])
            st["a3_cons"] = sV.inc(
                V.tensor_tensor(f32c[:, :], f32c[:, :], f32b[:, :], ALU.add))
            S_.wait_ge(sV.h, st["a3_cons"])
            S_.dma_start(bass.AP(x1sc, (s * S + tt * 128) * E, [[E, 128], [1, E]]),
                         f32c[:, :]).then_inc(sSC.h, 16)
            sSC.n += 16
            st["a3_st"] = sSC.n

    endA3 = dict(T=sT.n, V=sV.n, S=sS.n, X=sX.n, SC=sSC.n, DS=sDS.n)

    def fslot_wait(i):
        if st.get(f"fsl{i}") is not None:
            T.wait_ge(sS.h, st[f"fsl{i}"])

    for ft in range(NFT):
        for sub in range(NSUB):
            if ft == 0 and sub == 0:
                S_.wait_ge(sSC.h, endA3["SC"])
                S_.wait_ge(sV.h, endA3["V"])
            if st.get("f1_n") is not None:
                S_.wait_ge(sV.h, st["f1_n"])
            off = (ft * TFF + sub * 128) * E
            S_.dma_start(f32a[:, :], bass.AP(x1sc, off, [[E, 128], [1, E]])).then_inc(sDS.h, 16)
            sDS.n += 16
            st["f1_r"] = sDS.n
            S_.wait_ge(sDS.h, st["f1_r"])
            emit_ln(f32a[:, :], xhat_tok[:, 0:E])
            st["f1_n"] = st["ln_out"]
            X.wait_ge(sV.h, st["f1_n"])
            if sub == 0:
                if ft == 0:
                    X.wait_ge(sT.h, endA3["T"])
                elif st.get("f5_done") is not None:
                    X.wait_ge(sT.h, st["f5_done"])
            X.dma_start_transpose(
                xh2_fm[:, 0:8 * TFF].rearrange("p (ic sb c) -> p sb ic c",
                                               ic=8, sb=NSUB)[:, sub, :, :],
                xhat_tok[:, 0:E].rearrange("p (ic c) -> p ic c", ic=8)).then_inc(sX.h, 16)
            sX.n += 16
            st["f1_tr"] = sX.n

        for hc in range(32):
            wb = hc % 2
            key = f"f2_wu{wb}"
            if st.get(key) is not None:
                G.wait_ge(sT.h, st[key])
            if hc < 2 and st.get(f"f3_wu{wb}") is not None:
                G.wait_ge(sT.h, st[f"f3_wu{wb}"])
            for wi, wt in ((0, w1T_d), (1, w2T_d)):
                G.dma_start(w12_sb[wb][:, wi * 1024:(wi + 1) * 1024],
                            bass.AP(wt, hc * 128 * 1024,
                                    [[1024, 128], [1, 1024]])).then_inc(sDW.h, 16)
                sDW.n += 16
            st[f"f2_w{wb}"] = sDW.n
            for hf in range(NHALF):
                g2 = hc * NHALF + hf
                sl0 = g2 % 2
                sl1 = 2 + g2 % 2
                T.wait_ge(sDW.h, st[f"f2_w{wb}"])
                T.wait_ge(sX.h, st["f1_tr"])
                if ft == 0 and hc == 0 and hf == 0:
                    T.wait_ge(sV.h, endA3["V"])
                fslot_wait(sl0)
                fslot_wait(sl1)
                for wi, sl in ((0, sl0), (1, sl1)):
                    for ic in range(8):
                        hnd = T.matmul(ps_a[:, sl * 512:(sl + 1) * 512],
                                       w12_sb[wb][:, wi * 1024 + ic * 128:
                                                  wi * 1024 + (ic + 1) * 128],
                                       xh2_fm[:, ic * TFF + hf * 512:
                                              ic * TFF + (hf + 1) * 512],
                                       start=(ic == 0), stop=(ic == 7))
                st["f2_mm"] = sT.inc(hnd)
                if hf == NHALF - 1:
                    st[f"f2_wu{wb}"] = sT.n
                S_.wait_ge(sT.h, st["f2_mm"])
                if st.get("f2_h") is not None:
                    S_.wait_ge(sV.h, st["f2_h"])
                S_.activation(ab_sb[:, 0:512], ps_a[:, sl0 * 512:(sl0 + 1) * 512],
                              AF.Silu, scale=s1_sb[:, hc:hc + 1])
                hnd = S_.activation(ab_sb[:, 512:1024], ps_a[:, sl1 * 512:(sl1 + 1) * 512],
                                    AF.Copy, scale=s2_sb[:, hc:hc + 1])
                st[f"fsl{sl0}"] = st[f"fsl{sl1}"] = st["f2_ev"] = sS.inc(hnd)
                V.wait_ge(sS.h, st["f2_ev"])
                st["f2_h"] = sV.inc(
                    V.tensor_tensor(h_buf[:, hc * TFF + hf * 512:hc * TFF + (hf + 1) * 512],
                                    ab_sb[:, 0:512], ab_sb[:, 512:1024], ALU.mult))

        g3 = 0
        for stage, (wt_d, scol, src_buf, dst_buf, nout, rowd) in enumerate((
                (h1T_d, h1s_sb, h_buf, g1_buf, 32, FH),
                (h2T_d, h2s_sb, g1_buf, h_buf, 32, FH),
                (w3T_d, w3s_sb, h_buf, w3fm, 8, E))):
            af = AF.Gelu if stage < 2 else AF.Copy
            for oc in range(nout):
                wb = oc % 2
                key = f"f3_wu{wb}"
                if st.get(key) is not None:
                    G.wait_ge(sT.h, st[key])
                if stage == 0 and oc < 2 and st.get(f"f2_wu{wb}") is not None:
                    G.wait_ge(sT.h, st[f"f2_wu{wb}"])
                G.dma_start(hst_sb[wb][:, :],
                            bass.AP(wt_d, oc * 128 * 4096,
                                    [[4096, 128], [1, 4096]])).then_inc(sDW.h, 16)
                sDW.n += 16
                st[f"f3_w{wb}"] = sDW.n
                for hf in range(NHALF):
                    sl = g3 % 4
                    g3 += 1
                    T.wait_ge(sDW.h, st[f"f3_w{wb}"])
                    if stage == 0:
                        T.wait_ge(sV.h, st["f2_h"])
                    elif stage == 1:
                        T.wait_ge(sS.h, st["f3_ev0"])
                    else:
                        T.wait_ge(sS.h, st["f3_ev1"])
                    fslot_wait(sl)
                    for ic in range(32):
                        hnd = T.matmul(ps_a[:, sl * 512:(sl + 1) * 512],
                                       hst_sb[wb][:, ic * 128:(ic + 1) * 128],
                                       src_buf[:, ic * TFF + hf * 512:
                                               ic * TFF + (hf + 1) * 512],
                                       start=(ic == 0), stop=(ic == 31))
                    st["f3_mm"] = sT.inc(hnd)
                    if hf == NHALF - 1:
                        st[f"f3_wu{wb}"] = sT.n
                    if stage == 2 and oc == nout - 1 and hf == NHALF - 1:
                        st["f5_done"] = sT.n
                    S_.wait_ge(sT.h, st["f3_mm"])
                    if stage == 2 and oc == 0 and hf == 0 and st.get("f6_tr") is not None:
                        S_.wait_ge(sX.h, st["f6_tr"])
                    if stage == 2:
                        dst_ap = w3fm.rearrange("p (sb ic c) -> p ic sb c",
                                                sb=NSUB, ic=8)[:, oc, 4 * hf:4 * hf + 4, :]
                        hnd = S_.activation(dst_ap,
                                            ps_a[:, sl * 512:(sl + 1) * 512]
                                            .rearrange("p (sb c) -> p sb c", sb=4),
                                            af, scale=scol[:, oc:oc + 1])
                    else:
                        hnd = S_.activation(dst_buf[:, oc * TFF + hf * 512:
                                                    oc * TFF + (hf + 1) * 512],
                                            ps_a[:, sl * 512:(sl + 1) * 512],
                                            af, scale=scol[:, oc:oc + 1])
                    st[f"fsl{sl}"] = st[f"f3_ev{stage}"] = sS.inc(hnd)

        for sub in range(NSUB):
            X.wait_ge(sS.h, st["f3_ev2"])
            if st.get("f6_cp") is not None:
                X.wait_ge(sV.h, st["f6_cp"])
            X.dma_start_transpose(
                xhat_fm[:, 0:E].rearrange("p (ic c) -> p ic c", ic=8),
                w3fm[:, sub * E:(sub + 1) * E]
                .rearrange("p (ic c) -> p ic c", ic=8)).then_inc(sX.h, 16)
            sX.n += 16
            st["f6_tr"] = sX.n
            if st.get("f6_add") is not None:
                S_.wait_ge(sV.h, st["f6_add"])
            off = (ft * TFF + sub * 128) * E
            S_.dma_start(f32c[:, :], bass.AP(x1sc, off, [[E, 128], [1, E]])).then_inc(sDS.h, 16)
            sDS.n += 16
            st["f6_r"] = sDS.n
            V.wait_ge(sX.h, st["f6_tr"])
            if st.get("f6_st") is not None:
                V.wait_ge(sOUT.h, st["f6_st"])
            st["f6_cp"] = sV.inc(V.tensor_copy(f32b[:, :], xhat_fm[:, 0:E]))
            emit_ln(f32b[:, :], f32b[:, :], pre_wait_sv=st["f6_cp"])
            V.wait_ge(sV.h, st["ln_out"])
            V.wait_ge(sDS.h, st["f6_r"])
            st["f6_add"] = sV.inc(
                V.tensor_tensor(f32b[:, :], f32b[:, :], f32c[:, :], ALU.add))
            S_.wait_ge(sV.h, st["f6_add"])
            tok = ft * TFF + sub * 128
            S_.dma_start(out_d[tok // S, tok % S:tok % S + 128, :],
                         f32b[:, :]).then_inc(sOUT.h, 16)
            sOUT.n += 16
            st["f6_st"] = sOUT.n

    S_.wait_ge(sOUT.h, sOUT.n)

    def replay(eng, rec):
        for ent in rec.calls:
            hnd = getattr(eng, ent["op"])(*ent["a"], **ent["k"])
            for (semh, val) in ent["chain"]:
                hnd = hnd.then_inc(semh, val)

    with nc.Block() as block:
        @block.tensor
        def _(t):
            replay(t, T)

        @block.vector
        def _(v):
            replay(v, V)

        @block.scalar
        def _(s2):
            replay(s2, S_)

        @block.sync
        def _(x2):
            replay(x2, X)

        @block.gpsimd
        def _(g2):
            replay(g2, G)

    for cm in cms[::-1]:
        cm.__exit__(None, None, None)
    for sm in (sT, sV, sS, sG, sX, sDW, sDS, sSC, sOUT):
        sm._cm.__exit__(None, None, None)
    return nc


def prep_inputs(L0_attn_x, L0_attn_Wq, L0_attn_bq, L0_attn_Wv, L0_attn_bv,
                L0_attn_hyperplanes, L0_ffn_w1_w, L0_ffn_w1_s, L0_ffn_w2_w, L0_ffn_w2_s,
                L0_ffn_h1_w, L0_ffn_h1_s, L0_ffn_h2_w, L0_ffn_h2_s, L0_ffn_w3_w, L0_ffn_w3_s):
    def f(a):
        return np.ascontiguousarray(np.asarray(a, dtype=np.float32))
    def tile_in(wT):
        K_, M_ = wT.shape
        return np.ascontiguousarray(
            wT.reshape(K_ // 128, 128, M_ // 128, 128).transpose(2, 1, 0, 3))

    def tile_qv(wT):
        return np.ascontiguousarray(wT.reshape(8, 128, E).transpose(1, 0, 2))

    com = dict(
        wqT=f(tile_qv(np.asarray(L0_attn_Wq, dtype=np.float32).T)),
        wvT=f(tile_qv(np.asarray(L0_attn_Wv, dtype=np.float32).T)),
        bq=f(L0_attn_bq), bv=f(L0_attn_bv), hyp=f(L0_attn_hyperplanes),
        w1T=f(tile_in(np.asarray(L0_ffn_w1_w, dtype=np.float32).T)),
        w2T=f(tile_in(np.asarray(L0_ffn_w2_w, dtype=np.float32).T)),
        h1T=f(tile_in(np.asarray(L0_ffn_h1_w, dtype=np.float32).T)),
        h2T=f(tile_in(np.asarray(L0_ffn_h2_w, dtype=np.float32).T)),
        w3T=f(tile_in(np.asarray(L0_ffn_w3_w, dtype=np.float32).T)),
        s1=f(L0_ffn_w1_s), s2=f(L0_ffn_w2_s), h1s=f(L0_ffn_h1_s), h2s=f(L0_ffn_h2_s),
        w3s=f(L0_ffn_w3_s))
    x = f(L0_attn_x)
    return x, com


def kernel(**inputs):
    x, com = prep_inputs(**inputs)
    B, S, _ = x.shape
    ncore = 8
    nseq = B // ncore
    nc = build_program(nseq, S)
    in_maps = []
    for c in range(ncore):
        m = dict(com)
        m["x"] = np.ascontiguousarray(x[c * nseq:(c + 1) * nseq])
        in_maps.append(m)
    res = run_bass_kernel_spmd(nc, in_maps, core_ids=list(range(ncore)))
    out = np.concatenate([np.asarray(r["out"]) for r in res.results], axis=0)
    return out.reshape(B, S, E)


# version 10 (final)
# speedup vs baseline: 1.4202x; 1.0038x over previous
import numpy as np
import concourse.bass as bass
import concourse.mybir as mybir
from concourse.bass_utils import run_bass_kernel_spmd

F32 = mybir.dt.float32
BF16 = mybir.dt.bfloat16
AF = mybir.ActivationFunctionType
ALU = mybir.AluOpType
AX = mybir.AxisListType.X

E = 1024
H = 16
DH = 64
NH = 6
FH = 4096
ISQ32 = float(1.0 / np.sqrt(np.float32(32.0)))
EPS = 1e-5


class Rec:

    def __init__(self, name):
        self.name = name
        self.calls = []

    def __getattr__(self, op):
        if op.startswith("_"):
            raise AttributeError(op)
        calls = self.calls

        def call(*a, **k):
            ent = {"op": op, "a": a, "k": k, "chain": []}
            calls.append(ent)

            class Hnd:
                def then_inc(self, sem, val=1):
                    ent["chain"].append((sem, val))
                    return self
            return Hnd()
        return call


class Sem:
    def __init__(self, nc, name):
        self._cm = nc.semaphore(name)
        self.h = self._cm.__enter__()
        self.n = 0

    def inc(self, _hnd, k=1):
        _hnd.then_inc(self.h, k)
        self.n += k
        return self.n


def build_program(nseq, S, debug=False):
    nc = bass.Bass("TRN2", target_bir_lowering=False)
    NT = nseq * S
    QT = S // 128
    KC = S // 128
    GR = S // 256
    NHEAD = nseq * H
    NQG = NHEAD * QT
    TFF = min(1024, NT)
    NFT = NT // TFF
    NSUB = TFF // 128
    NHALF = TFF // 512

    x_in = nc.dram_tensor("x", [nseq, S, E], F32, kind="ExternalInput")
    wqT_d = nc.dram_tensor("wqT", [128, 8, E], F32, kind="ExternalInput")
    wvT_d = nc.dram_tensor("wvT", [128, 8, E], F32, kind="ExternalInput")
    bq_d = nc.dram_tensor("bq", [E], F32, kind="ExternalInput")
    bv_d = nc.dram_tensor("bv", [E], F32, kind="ExternalInput")
    hyp_d = nc.dram_tensor("hyp", [65, NH], F32, kind="ExternalInput")
    w1T_d = nc.dram_tensor("w1T", [32, 128, 8, 128], F32, kind="ExternalInput")
    w2T_d = nc.dram_tensor("w2T", [32, 128, 8, 128], F32, kind="ExternalInput")
    h1T_d = nc.dram_tensor("h1T", [32, 128, 32, 128], F32, kind="ExternalInput")
    h2T_d = nc.dram_tensor("h2T", [32, 128, 32, 128], F32, kind="ExternalInput")
    w3T_d = nc.dram_tensor("w3T", [8, 128, 32, 128], F32, kind="ExternalInput")
    s1_d = nc.dram_tensor("s1", [FH], F32, kind="ExternalInput")
    s2_d = nc.dram_tensor("s2", [FH], F32, kind="ExternalInput")
    h1s_d = nc.dram_tensor("h1s", [FH], F32, kind="ExternalInput")
    h2s_d = nc.dram_tensor("h2s", [FH], F32, kind="ExternalInput")
    w3s_d = nc.dram_tensor("w3s", [E], F32, kind="ExternalInput")
    out_d = nc.dram_tensor("out", [nseq, S, E], F32, kind="ExternalOutput")
    qsc = nc.dram_tensor("qsc", [nseq, 8, 128, S], BF16)
    vsc = nc.dram_tensor("vsc", [nseq, H, 128, KC, DH], BF16)
    attsc = nc.dram_tensor("attsc", [nseq, S, E], F32)
    x1sc = nc.dram_tensor("x1sc", [nseq, S, E], F32)

    cms = []

    def sbuf(name, shape, dt):
        cm = nc.sbuf_tensor(name, shape, dt)
        t = cm.__enter__()
        cms.append(cm)
        return t

    def psum(name, shape, dt=F32):
        cm = nc.psum_tensor(name, shape, dt)
        t = cm.__enter__()
        cms.append(cm)
        return t

    ptg3 = sbuf("ptg3", [128, 4096], BF16)
    iota_p = sbuf("iota_p", [128, 1], F32)
    hyp_sb = sbuf("hyp_sb", [64, NH], BF16)
    hb_col = sbuf("hb_col", [NH, 1], F32)
    pw2 = sbuf("pw2", [NH, 1], BF16)
    ones_col = sbuf("ones_col", [128, 1], BF16)
    neg63_col = sbuf("neg63_col", [128, 1], BF16)
    ones_row = sbuf("ones_row", [1, 128], BF16)
    bq_sb = sbuf("bq_sb", [128, 8], F32)
    bv_sb = sbuf("bv_sb", [128, 8], F32)
    s1_sb = sbuf("s1_sb", [128, 32], F32)
    s2_sb = sbuf("s2_sb", [128, 32], F32)
    h1s_sb = sbuf("h1s_sb", [128, 32], F32)
    h2s_sb = sbuf("h2s_sb", [128, 32], F32)
    w3s_sb = sbuf("w3s_sb", [128, 8], F32)
    lncol = sbuf("lncol", [128, 8], F32)
    biascol_sb = sbuf("biascol_sb", [128, max(QT, 1)], F32)
    denbuf = sbuf("denbuf", [128, max(QT, 4)], F32)
    denr = sbuf("denr", [128, max(QT, 4)], F32)
    bits_sb = sbuf("bits_sb", [NH, 512], BF16)
    bucket_sb = sbuf("bucket_sb", [1, 512], BF16)
    bqall = sbuf("bqall", [128, max(QT, 1)], F32)

    big0 = sbuf("big0", [128, 32768], BF16)
    big1 = sbuf("big1", [128, 32768], BF16)
    wq_sb = sbuf("wq_sb", [128, 8192], BF16)
    wv_sb = sbuf("wv_sb", [128, 8192], BF16)
    hst_sb = [sbuf("hsta", [128, 4096], BF16), sbuf("hstb", [128, 4096], BF16)]
    w12_sb = [hst_sb[0][:, 0:2048], hst_sb[1][:, 0:2048]]
    ab_sb = sbuf("ab_sb", [128, 1024], BF16)
    f32ar = sbuf("f32ar", [128, 4096], F32)

    ptg = [big0[:, 0:4096], big0[:, 4096:8192], ptg3[:, :]]
    qbuf = [big0[:, 8192:12288], big0[:, 12288:16384]]
    zbuf = [big0[:, 16384:20480], big0[:, 20480:24576]]
    pbuf = [big0[:, 24576:28672], big0[:, 28672:32768]]
    sq_sb = big1[:, 0:4096]
    bcb_sb = big1[:, 4096:8192]
    ctile = big1[:, 8192:12288]
    vbuf = [big1[:, 12288:14336], big1[:, 14336:16384]]
    xhat_tok = big1[:, 16384:18432]
    xhat_fm = big1[:, 18432:20480]
    vfm_grp = big1[:, 20480:22528]
    vtok_grp = big1[:, 22528:24576]
    qev = [big1[:, 24576:24832], big1[:, 24832:25088]]
    h_buf = big0
    g1_buf = big1
    xh2_fm = wq_sb
    w3fm = wv_sb
    xbuf = f32ar[:, 0:2048]
    xsq = f32ar[:, 2048:3072]
    attsl = f32ar[:, 3584:3840]
    f32a = f32ar[:, 0:1024]
    f32b = f32ar[:, 1024:2048]
    f32c = f32ar[:, 3072:4096]

    ps_a = psum("ps_a", [128, 2048])
    ps_b = psum("ps_b", [128, 1024])
    ps_c = psum("ps_c", [128, 512])
    ps_d = psum("ps_d", [128, 512])

    sT = Sem(nc, "sT")
    sV = Sem(nc, "sV")
    sS = Sem(nc, "sS")
    sG = Sem(nc, "sG")
    sX = Sem(nc, "sX")
    sDW = Sem(nc, "sDW")
    sDS = Sem(nc, "sDS")
    sSC = Sem(nc, "sSC")
    sOUT = Sem(nc, "sOUT")

    T, V, S_, X, G = Rec("T"), Rec("V"), Rec("S"), Rec("X"), Rec("G")

    G.iota(iota_p[:, :], [[0, 1]], channel_multiplier=1,
           allow_small_or_imprecise_dtypes=True)
    G.memset(ones_col[:, :], 1.0)
    G.memset(neg63_col[:, :], -63.0)
    g_init = sG.inc(G.memset(ones_row[:, :], 1.0))

    G.dma_start(wq_sb[:, :], bass.AP(wqT_d, 0, [[8 * E, 128], [1, 8 * E]])).then_inc(sDW.h, 16)
    sDW.n += 16
    G.dma_start(wv_sb[:, :], bass.AP(wvT_d, 0, [[8 * E, 128], [1, 8 * E]])).then_inc(sDW.h, 16)
    sDW.n += 16
    w_qv = sDW.n
    for t_, src_ in ((hyp_sb, hyp_d[0:64, :]),
                     (hb_col, hyp_d[64:65, :].rearrange("a n -> n a")),
                     (bq_sb, bq_d[:].rearrange("(c p) -> p c", p=128)),
                     (bv_sb, bv_d[:].rearrange("(c p) -> p c", p=128)),
                     (s1_sb, s1_d[:].rearrange("(c p) -> p c", p=128)),
                     (s2_sb, s2_d[:].rearrange("(c p) -> p c", p=128)),
                     (h1s_sb, h1s_d[:].rearrange("(c p) -> p c", p=128)),
                     (h2s_sb, h2s_d[:].rearrange("(c p) -> p c", p=128)),
                     (w3s_sb, w3s_d[:].rearrange("(c p) -> p c", p=128))):
        G.dma_start(t_[:, :], src_, allow_slow_non_contiguous=True).then_inc(sDW.h, 16)
        sDW.n += 16
    w_misc = sDW.n

    V.wait_ge(sG.h, g_init)
    V.wait_ge(sDW.h, w_misc)
    V.tensor_scalar(bq_sb[:, :], bq_sb[:, :], ISQ32, None, ALU.mult)
    v_init = sV.inc(V.tensor_scalar(hb_col[:, :], hb_col[:, :], ISQ32, None, ALU.mult))
    T.wait_ge(sV.h, v_init)
    S_.wait_ge(sV.h, v_init)
    S_.wait_ge(sG.h, g_init)
    sS.inc(S_.activation(pw2[0:NH, :], iota_p[0:NH, :], AF.Exp,
                         scale=float(np.log(2.0))))

    st = {}

    def vstep(hnd):
        n = sV.inc(hnd)
        V.wait_ge(sV.h, n)
        return n

    def emit_ln(src_ap, dst_ap, pre_wait_sv=None):
        if pre_wait_sv is not None:
            S_.wait_ge(sV.h, pre_wait_sv)
        if st.get("ln_rd") is not None:
            S_.wait_ge(sV.h, st["ln_rd"])
        st["ln_sq"] = sS.inc(S_.activation(xsq[:, :], src_ap, AF.Square))
        V.wait_ge(sS.h, st["ln_sq"])
        V.reduce_sum(lncol[:, 0:1], src_ap, axis=AX)
        vstep(V.reduce_sum(lncol[:, 1:2], xsq[:, :], axis=AX))
        vstep(V.tensor_scalar(lncol[:, 2:3], lncol[:, 0:1], 1.0 / E, None, ALU.mult))
        vstep(V.tensor_tensor(lncol[:, 3:4], lncol[:, 2:3], lncol[:, 2:3], ALU.mult))
        vstep(V.tensor_scalar(lncol[:, 4:5], lncol[:, 1:2], 1.0 / E, lncol[:, 3:4],
                              ALU.mult, ALU.subtract))
        st["ln_var"] = sV.inc(
            V.tensor_scalar(lncol[:, 4:5], lncol[:, 4:5], EPS, None, ALU.add))
        st["ln_rd"] = st["ln_var"]
        S_.wait_ge(sV.h, st["ln_var"])
        st["ln_sd"] = sS.inc(S_.activation(lncol[:, 5:6], lncol[:, 4:5], AF.Sqrt))
        V.wait_ge(sS.h, st["ln_sd"])
        vstep(V.reciprocal(lncol[:, 6:7], lncol[:, 5:6]))
        st["ln_out"] = sV.inc(
            V.tensor_scalar(dst_ap, src_ap, lncol[:, 2:3], lncol[:, 6:7],
                            ALU.subtract, ALU.mult))
        return st["ln_out"]

    for s in range(nseq):
        for g in range(GR):
            t0 = g * 256
            if st.get("a1_xcons") is not None:
                S_.wait_ge(sV.h, st["a1_xcons"])
            src = x_in[s, t0:t0 + 256, :].rearrange("(j p) f -> p j f", p=128)
            S_.dma_start(xbuf[:, :].rearrange("p (j f) -> p j f", j=2), src).then_inc(sDS.h, 16)
            sDS.n += 16
            st["a1_xr"] = sDS.n
            for j in range(2):
                S_.wait_ge(sDS.h, st["a1_xr"])
                ln_out = emit_ln(xbuf[:, j * E:(j + 1) * E],
                                 xhat_tok[:, j * E:(j + 1) * E])
                if j == 1:
                    st["a1_xcons"] = ln_out
            X.wait_ge(sV.h, st["ln_out"])
            if st.get("a1_projdone") is not None:
                X.wait_ge(sT.h, st["a1_projdone"])
            for j in range(2):
                X.dma_start_transpose(
                    xhat_fm.rearrange("p (ic j c) -> p ic j c", ic=8, j=2)[:, :, j, :],
                    xhat_tok[:, j * E:(j + 1) * E]
                    .rearrange("p (ic c) -> p ic c", ic=8)).then_inc(sX.h, 16)
                sX.n += 16
            st["a1_xfm"] = sX.n

            for wi, (w_sb, bias_sb, scl) in enumerate(
                    ((wq_sb, bq_sb, ISQ32), (wv_sb, bv_sb, 1.0))):
                for oc in range(8):
                    slot_off = wi * 1024 + (oc % 2) * 512
                    pst = ps_a[:, slot_off:slot_off + 256]
                    T.wait_ge(sX.h, st["a1_xfm"])
                    if wi == 0 and oc == 0:
                        T.wait_ge(sDW.h, w_qv)
                    key = f"a1_sl{wi}{oc % 2}"
                    if st.get(key) is not None:
                        T.wait_ge(sV.h, st[key])
                    for ic in range(8):
                        hnd = T.matmul(pst,
                                       w_sb[:, ic * E + oc * 128: ic * E + (oc + 1) * 128],
                                       xhat_fm[:, ic * 256:(ic + 1) * 256],
                                       start=(ic == 0), stop=(ic == 7))
                    st[f"a1_mm{wi}"] = sT.inc(hnd)
                    if wi == 1 and oc == 7:
                        st["a1_projdone"] = sT.n
                    V.wait_ge(sT.h, st[f"a1_mm{wi}"])
                    if wi == 0:
                        qv = qev[oc % 2]
                        if st.get(f"a1_qst{oc % 2}") is not None:
                            V.wait_ge(sSC.h, st[f"a1_qst{oc % 2}"])
                        st[key] = sV.inc(
                            V.tensor_scalar(qv[:, :], pst,
                                            bias_sb[:, oc:oc + 1], scl,
                                            ALU.add, ALU.mult))
                        S_.wait_ge(sV.h, st[key])
                        S_.dma_start(qsc[s, oc, :, t0:t0 + 256], qv[:, :]).then_inc(sSC.h, 16)
                        sSC.n += 16
                        st[f"a1_qst{oc % 2}"] = sSC.n
                    else:
                        if st.get("a1_vtr") is not None and oc == 0:
                            V.wait_ge(sX.h, st["a1_vtr"])
                        st[key] = sV.inc(
                            V.tensor_scalar(
                                vfm_grp.rearrange("p (tj oc c) -> p oc tj c",
                                                  tj=2, oc=8)[:, oc, :, :],
                                pst.rearrange("p (tj c) -> p tj c", tj=2),
                                bias_sb[:, oc:oc + 1], None, ALU.add))
                        st["a1_ev1"] = st[key]
            X.wait_ge(sV.h, st["a1_ev1"])
            if st.get("a1_vst") is not None:
                X.wait_ge(sDS.h, st["a1_vst"])
            for tj in range(2):
                X.dma_start_transpose(
                    vtok_grp[:, tj * 1024:(tj + 1) * 1024]
                    .rearrange("p (oc c) -> p oc c", oc=8),
                    vfm_grp[:, tj * 1024:(tj + 1) * 1024]
                    .rearrange("p (oc c) -> p oc c", oc=8)).then_inc(sX.h, 16)
                sX.n += 16
            st["a1_vtr"] = sX.n
            S_.wait_ge(sX.h, st["a1_vtr"])
            for tj in range(2):
                dst = bass.AP(vsc, s * H * 128 * KC * DH + (g * 2 + tj) * DH,
                              [[KC * DH, 128], [128 * KC * DH, 16], [1, DH]])
                S_.dma_start(dst, vtok_grp[:, tj * 1024:(tj + 1) * 1024]
                             .rearrange("p (hh d) -> p hh d", hh=16)).then_inc(sDS.h, 16)
                sDS.n += 16
            st["a1_vst"] = sDS.n

    endA1 = dict(T=sT.n, V=sV.n, S=sS.n, X=sX.n, SC=sSC.n, DS=sDS.n)

    for qg in range(NQG + 8):
        def emit_pv(qp):
            hip = qp // QT
            bufp = hip % 2
            if st.get(f"a2_evac{qp % 4}") is not None:
                T.wait_ge(sS.h, st[f"a2_evac{qp % 4}"])
            for m in range(KC):
                if m % (KC // 4) == 0:
                    T.wait_ge(sX.h, st[f"a2_ptrp{qp % 3}_{m // (KC // 4)}"])
                hnd = T.matmul(ps_d[:, (qp % 4) * 64:(qp % 4) * 64 + 64],
                               ptg[qp % 3][:, m * 128:(m + 1) * 128],
                               vbuf[bufp][:, m * DH:(m + 1) * DH],
                               start=(m == 0), stop=(m == KC - 1))
            st[f"a2_pv{qp % 4}"] = sT.inc(hnd)
            if qp % QT == QT - 1:
                st[f"a2_vfree{bufp}"] = sT.n

        if qg >= NQG and qg - 1 < NQG:
            emit_pv(qg - 1)

        if qg < NQG:
            hi = qg // QT
            qt = qg % QT
            s, h = hi // H, hi % H
            buf = hi % 2
            zt = qg % 2
            if qt == 0:
                if hi == 0:
                    G.wait_ge(sSC.h, endA1["SC"])
                    G.wait_ge(sDS.h, endA1["DS"])
                if st.get(f"a2_qfree{buf}") is not None:
                    G.wait_ge(sT.h, st[f"a2_qfree{buf}"])
                if st.get(f"a2_vfree{buf}") is not None:
                    G.wait_ge(sT.h, st[f"a2_vfree{buf}"])
                G.dma_start(qbuf[buf][0:64, :],
                            qsc[s, h // 2, 64 * (h % 2):64 * (h % 2) + 64, :]).then_inc(sDS.h, 16)
                sDS.n += 16
                G.dma_start(vbuf[buf],
                            bass.AP(vsc, (s * H + h) * 128 * KC * DH,
                                    [[KC * DH, 128], [1, KC * DH]])).then_inc(sDS.h, 16)
                sDS.n += 16
                st[f"a2_qv{buf}"] = sDS.n
                S_.wait_ge(sDS.h, st[f"a2_qv{buf}"])
                if st.get("a2_sqfree") is not None:
                    S_.wait_ge(sT.h, st["a2_sqfree"])
                st["a2_sq"] = sS.inc(S_.activation(sq_sb[0:64, :], qbuf[buf][0:64, :],
                                                   AF.Square))
                T.wait_ge(sS.h, st["a2_sq"])
                if st.get("a2_bcp") is not None:
                    T.wait_ge(sV.h, st["a2_bcp"])
                for qq in range(QT):
                    hnd = T.matmul(ps_d[:, 256 + qq:257 + qq],
                                   sq_sb[0:64, qq * 128:(qq + 1) * 128],
                                   neg63_col[0:64, :], start=True, stop=True)
                st["a2_s2b"] = sT.inc(hnd)
                V.wait_ge(sT.h, st["a2_s2b"])
                if st.get("a2_expl") is not None:
                    V.wait_ge(sS.h, st["a2_expl"])
                st["a2_bcp"] = sV.inc(V.tensor_copy(biascol_sb[:, :],
                                                    ps_d[:, 256:256 + QT]))
                for ch in range(S // 512):
                    T.wait_ge(sDS.h, st[f"a2_qv{buf}"])
                    if hi == 0 and ch == 0:
                        T.wait_ge(sS.h, endA1["S"])
                        T.wait_ge(sV.h, endA1["V"])
                    if ch == 0 and st.get("a2_z3") is not None:
                        T.wait_ge(sV.h, st["a2_z3"])
                    if st.get("a2_bits") is not None:
                        T.wait_ge(sV.h, st["a2_bits"])
                    hnd = T.matmul(ps_b[0:NH, 0:512], hyp_sb[:, :],
                                   qbuf[buf][0:64, ch * 512:(ch + 1) * 512],
                                   start=True, stop=True)
                    st["a2_bp"] = sT.inc(hnd)
                    V.wait_ge(sT.h, st["a2_bp"])
                    if ch == 0 and st.get("a2_ctG") is not None:
                        V.wait_ge(sG.h, st["a2_ctG"])
                    if st.get("a2_sum") is not None:
                        V.wait_ge(sT.h, st["a2_sum"])
                    st["a2_bits"] = sV.inc(
                        V.tensor_scalar(bits_sb[:, :], ps_b[0:NH, 0:512],
                                        hb_col[:, 0:1], 0.0, ALU.add, ALU.is_ge))
                    T.wait_ge(sV.h, st["a2_bits"])
                    if st.get("a2_bcbev") is not None:
                        T.wait_ge(sV.h, st["a2_bcbev"])
                    hnd = T.matmul(ps_c[0:1, 0:512], pw2[:, :],
                                   bits_sb[:, :], start=True, stop=True)
                    if st.get("a2_bqc") is not None:
                        T.wait_ge(sV.h, st["a2_bqc"])
                    for i4 in range(4):
                        hnd = T.matmul(ps_d[:, 288 + i4:289 + i4],
                                       bits_sb[:, i4 * 128:(i4 + 1) * 128], pw2[:, :],
                                       start=True, stop=True)
                    st["a2_sum"] = sT.inc(hnd)
                    V.wait_ge(sT.h, st["a2_sum"])
                    V.tensor_copy(bucket_sb[0:1, :], ps_c[0:1, 0:512])
                    st["a2_bqc"] = sV.inc(
                        V.tensor_copy(bqall[:, ch * 4:(ch + 1) * 4], ps_d[:, 288:292]))
                    T.wait_ge(sV.h, st["a2_bqc"])
                    hnd = T.matmul(ps_c[0:128, 0:512], ones_row[0:1, :],
                                   bucket_sb[0:1, :], start=True, stop=True)
                    st["a2_bc"] = sT.inc(hnd)
                    V.wait_ge(sT.h, st["a2_bc"])
                    st["a2_bcbev"] = sV.inc(
                        V.tensor_copy(bcb_sb[:, ch * 512:(ch + 1) * 512],
                                      ps_c[0:128, 0:512]))
                st["a2_bcb"] = st["a2_bcbev"]

            V.wait_ge(sV.h, st["a2_bcb"])
            st["a2_ct"] = ("V", sV.inc(
                V.tensor_scalar(ctile[:, :], bcb_sb[:, :], bqall[:, qt:qt + 1],
                                62.0, ALU.is_equal, ALU.add)))

            if st.get("a2_z1") is not None:
                T.wait_ge(sV.h, st["a2_z1"])
            for kk in range(4):
                hnd = T.matmul(ps_a[:, kk * 512:(kk + 1) * 512],
                               qbuf[buf][0:64, qt * 128:(qt + 1) * 128],
                               qbuf[buf][0:64, kk * 512:(kk + 1) * 512],
                               start=True, stop=True)
            st["a2_qkA"] = sT.inc(hnd)
            if qt == 0:
                T.wait_ge(sV.h, st["a2_bits"])
            if st.get("a2_z2") is not None:
                T.wait_ge(sV.h, st["a2_z2"])
            for kk in range(2):
                hnd = T.matmul(ps_b[:, kk * 512:(kk + 1) * 512],
                               qbuf[buf][0:64, qt * 128:(qt + 1) * 128],
                               qbuf[buf][0:64, 2048 + kk * 512:2048 + (kk + 1) * 512],
                               start=True, stop=True)
            st["a2_qkB"] = sT.inc(hnd)

            if qg - 1 >= 0:
                emit_pv(qg - 1)

            ctl, ctn = st["a2_ct"]
            if ctl == "G":
                V.wait_ge(sG.h, ctn)
            else:
                V.wait_ge(sV.h, ctn)
            V.wait_ge(sT.h, st["a2_qkA"])
            if st.get(f"a2_expz{zt}") is not None:
                V.wait_ge(sS.h, st[f"a2_expz{zt}"])
            st["a2_z1"] = sV.inc(
                V.tensor_tensor(zbuf[zt][:, 0:2048], ctile[:, 0:2048],
                                ps_a[:, 0:2048], ALU.mult))
            V.wait_ge(sT.h, st["a2_qkB"])
            st["a2_z2"] = sV.inc(
                V.tensor_tensor(zbuf[zt][:, 2048:3072], ctile[:, 2048:3072],
                                ps_b[:, 0:1024], ALU.mult))

            T.wait_ge(sV.h, st["a2_z2"])
            for kk in range(2):
                hnd = T.matmul(ps_b[:, kk * 512:(kk + 1) * 512],
                               qbuf[buf][0:64, qt * 128:(qt + 1) * 128],
                               qbuf[buf][0:64, 3072 + kk * 512:3072 + (kk + 1) * 512],
                               start=True, stop=True)
            st["a2_qkB2"] = sT.inc(hnd)
            if qt == QT - 1:
                st[f"a2_qfree{buf}"] = sT.n
                st["a2_sqfree"] = sT.n

            V.wait_ge(sT.h, st["a2_qkB2"])
            st["a2_z3"] = sV.inc(
                V.tensor_tensor(zbuf[zt][:, 3072:4096], ctile[:, 3072:4096],
                                ps_b[:, 0:1024], ALU.mult))

        if qg % 4 == 0 and 0 < qg <= NQG:
            lo = qg - 4
            V.wait_ge(sS.h, st[f"a2_expn{(qg - 1) % 2}"])
            st["a2_den"] = sV.inc(
                V.reciprocal(denr[:, lo % QT:lo % QT + 4],
                             denbuf[:, lo % QT:lo % QT + 4]))

        if 0 <= qg - 4 < NQG:
            qe = qg - 4
            S_.wait_ge(sT.h, st[f"a2_pv{qe % 4}"])
            S_.wait_ge(sV.h, st["a2_den"])
            if qe % 4 == 0 and st.get("a2_store") is not None:
                S_.wait_ge(sSC.h, st["a2_store"])
            hnd = S_.activation(attsl[:, (qe % 4) * 64:(qe % 4) * 64 + 64],
                                ps_d[:, (qe % 4) * 64:(qe % 4) * 64 + 64],
                                AF.Copy, scale=denr[:, qe % QT:qe % QT + 1])
            st[f"a2_evac{qe % 4}"] = sS.inc(hnd)
            if qe % 4 == 3:
                G.wait_ge(sS.h, st[f"a2_evac{qe % 4}"])
                hie = qe // QT
                se, he = hie // H, hie % H
                qs = (qe % QT) // 4
                dst = bass.AP(attsc, (se * S + qs * 512) * E + he * DH,
                              [[E, 128], [128 * E, 4], [1, DH]])
                G.dma_start(dst, attsl[:, :]
                            .rearrange("p (j d) -> p j d", j=4)).then_inc(sSC.h, 16)
                sSC.n += 16
                st["a2_store"] = sSC.n

        if qg < NQG:
            S_.wait_ge(sV.h, st["a2_z3"])
            if qg >= 2:
                S_.wait_ge(sX.h, st[f"a2_ptr{(qg - 2) % 3}"])
            hnd = S_.activation(pbuf[zt][:, :], zbuf[zt][:, :], AF.Exp,
                                bias=biascol_sb[:, qt:qt + 1], scale=1.0,
                                accum_out=denbuf[:, qt:qt + 1])
            n = sS.inc(hnd)
            st[f"a2_expz{zt}"] = n
            st[f"a2_expn{qg % 2}"] = n
            if qt == QT - 1:
                st["a2_expl"] = n
            X.wait_ge(sS.h, n)
            if qg >= 3:
                X.wait_ge(sT.h, st[f"a2_pv{(qg - 3) % 4}"])
            for pc in range(4):
                X.dma_start_transpose(
                    ptg[qg % 3].rearrange("p (pc m c) -> p pc m c",
                                          pc=4, m=KC // 4)[:, pc, :, :],
                    pbuf[zt].rearrange("p (pc m c) -> p pc m c",
                                       pc=4, m=KC // 4)[:, pc, :, :]).then_inc(sX.h, 16)
                sX.n += 16
                st[f"a2_ptrp{qg % 3}_{pc}"] = sX.n
            st[f"a2_ptr{qg % 3}"] = sX.n

    endA2 = dict(T=sT.n, V=sV.n, S=sS.n, X=sX.n, SC=sSC.n, DS=sDS.n)

    endA3 = endA2

    def fslot_wait(i):
        if st.get(f"fsl{i}") is not None:
            T.wait_ge(sS.h, st[f"fsl{i}"])

    for ft in range(NFT):
        for sub in range(NSUB):
            tok0 = ft * TFF + sub * 128
            off = tok0 * E
            if ft == 0 and sub == 0:
                S_.wait_ge(sSC.h, endA3["SC"])
                S_.wait_ge(sV.h, endA3["V"])
            if st.get("f1_lnA") is not None:
                S_.wait_ge(sV.h, st["f1_lnA"])
            if st.get("f1_add") is not None:
                S_.wait_ge(sV.h, st["f1_add"])
            if sub == 0 and st.get("f6_st") is not None:
                S_.wait_ge(sOUT.h, st["f6_st"])
            S_.dma_start(f32a[:, :], bass.AP(attsc, off, [[E, 128], [1, E]])).then_inc(sDS.h, 16)
            sDS.n += 16
            S_.dma_start(f32b[:, :],
                         x_in[tok0 // S, tok0 % S:tok0 % S + 128, :]).then_inc(sDS.h, 16)
            sDS.n += 16
            st["f1_r"] = sDS.n
            S_.wait_ge(sDS.h, st["f1_r"])
            if st.get("f1_x1st") is not None:
                V.wait_ge(sSC.h, st["f1_x1st"])
            emit_ln(f32a[:, :], f32c[:, :])
            st["f1_lnA"] = st["ln_out"]
            V.wait_ge(sV.h, st["ln_out"])
            st["f1_add"] = sV.inc(
                V.tensor_tensor(f32c[:, :], f32c[:, :], f32b[:, :], ALU.add))
            S_.wait_ge(sV.h, st["f1_add"])
            S_.dma_start(bass.AP(x1sc, off, [[E, 128], [1, E]]),
                         f32c[:, :]).then_inc(sSC.h, 16)
            sSC.n += 16
            st["f1_x1st"] = sSC.n
            emit_ln(f32c[:, :], xhat_tok[:, 0:E], pre_wait_sv=st["f1_add"])
            st["f1_n"] = st["ln_out"]
            X.wait_ge(sV.h, st["f1_n"])
            if sub == 0:
                if ft == 0:
                    X.wait_ge(sT.h, endA3["T"])
                elif st.get("f5_done") is not None:
                    X.wait_ge(sT.h, st["f5_done"])
            X.dma_start_transpose(
                xh2_fm[:, 0:8 * TFF].rearrange("p (ic sb c) -> p sb ic c",
                                               ic=8, sb=NSUB)[:, sub, :, :],
                xhat_tok[:, 0:E].rearrange("p (ic c) -> p ic c", ic=8)).then_inc(sX.h, 16)
            sX.n += 16
            st["f1_tr"] = sX.n

        for hc in range(32):
            wb = hc % 2
            key = f"f2_wu{wb}"
            if st.get(key) is not None:
                G.wait_ge(sT.h, st[key])
            if hc < 2 and st.get(f"f3_wu{wb}") is not None:
                G.wait_ge(sT.h, st[f"f3_wu{wb}"])
            for wi, wt in ((0, w1T_d), (1, w2T_d)):
                G.dma_start(w12_sb[wb][:, wi * 1024:(wi + 1) * 1024],
                            bass.AP(wt, hc * 128 * 1024,
                                    [[1024, 128], [1, 1024]])).then_inc(sDW.h, 16)
                sDW.n += 16
            st[f"f2_w{wb}"] = sDW.n
            for hf in range(NHALF):
                g2 = hc * NHALF + hf
                sl0 = g2 % 2
                sl1 = 2 + g2 % 2
                T.wait_ge(sDW.h, st[f"f2_w{wb}"])
                T.wait_ge(sX.h, st["f1_tr"])
                if ft == 0 and hc == 0 and hf == 0:
                    T.wait_ge(sV.h, endA3["V"])
                fslot_wait(sl0)
                fslot_wait(sl1)
                for wi, sl in ((0, sl0), (1, sl1)):
                    for ic in range(8):
                        hnd = T.matmul(ps_a[:, sl * 512:(sl + 1) * 512],
                                       w12_sb[wb][:, wi * 1024 + ic * 128:
                                                  wi * 1024 + (ic + 1) * 128],
                                       xh2_fm[:, ic * TFF + hf * 512:
                                              ic * TFF + (hf + 1) * 512],
                                       start=(ic == 0), stop=(ic == 7))
                st["f2_mm"] = sT.inc(hnd)
                if hf == NHALF - 1:
                    st[f"f2_wu{wb}"] = sT.n
                S_.wait_ge(sT.h, st["f2_mm"])
                if st.get("f2_h") is not None:
                    S_.wait_ge(sV.h, st["f2_h"])
                S_.activation(ab_sb[:, 0:512], ps_a[:, sl0 * 512:(sl0 + 1) * 512],
                              AF.Silu, scale=s1_sb[:, hc:hc + 1])
                hnd = S_.activation(ab_sb[:, 512:1024], ps_a[:, sl1 * 512:(sl1 + 1) * 512],
                                    AF.Copy, scale=s2_sb[:, hc:hc + 1])
                st[f"fsl{sl0}"] = st[f"fsl{sl1}"] = st["f2_ev"] = sS.inc(hnd)
                V.wait_ge(sS.h, st["f2_ev"])
                st["f2_h"] = sV.inc(
                    V.tensor_tensor(h_buf[:, hc * TFF + hf * 512:hc * TFF + (hf + 1) * 512],
                                    ab_sb[:, 0:512], ab_sb[:, 512:1024], ALU.mult))

        g3 = 0
        for stage, (wt_d, scol, src_buf, dst_buf, nout, rowd) in enumerate((
                (h1T_d, h1s_sb, h_buf, g1_buf, 32, FH),
                (h2T_d, h2s_sb, g1_buf, h_buf, 32, FH),
                (w3T_d, w3s_sb, h_buf, w3fm, 8, E))):
            af = AF.Gelu if stage < 2 else AF.Copy
            for oc in range(nout):
                wb = oc % 2
                key = f"f3_wu{wb}"
                if st.get(key) is not None:
                    G.wait_ge(sT.h, st[key])
                if stage == 0 and oc < 2 and st.get(f"f2_wu{wb}") is not None:
                    G.wait_ge(sT.h, st[f"f2_wu{wb}"])
                G.dma_start(hst_sb[wb][:, :],
                            bass.AP(wt_d, oc * 128 * 4096,
                                    [[4096, 128], [1, 4096]])).then_inc(sDW.h, 16)
                sDW.n += 16
                st[f"f3_w{wb}"] = sDW.n
                for hf in range(NHALF):
                    sl = g3 % 4
                    g3 += 1
                    T.wait_ge(sDW.h, st[f"f3_w{wb}"])
                    if stage == 0:
                        T.wait_ge(sV.h, st["f2_h"])
                    elif stage == 1:
                        T.wait_ge(sS.h, st["f3_ev0"])
                    else:
                        T.wait_ge(sS.h, st["f3_ev1"])
                    fslot_wait(sl)
                    for ic in range(32):
                        hnd = T.matmul(ps_a[:, sl * 512:(sl + 1) * 512],
                                       hst_sb[wb][:, ic * 128:(ic + 1) * 128],
                                       src_buf[:, ic * TFF + hf * 512:
                                               ic * TFF + (hf + 1) * 512],
                                       start=(ic == 0), stop=(ic == 31))
                    st["f3_mm"] = sT.inc(hnd)
                    if hf == NHALF - 1:
                        st[f"f3_wu{wb}"] = sT.n
                    if stage == 2 and oc == nout - 1 and hf == NHALF - 1:
                        st["f5_done"] = sT.n
                    S_.wait_ge(sT.h, st["f3_mm"])
                    if stage == 2 and oc == 0 and hf == 0 and st.get("f6_tr") is not None:
                        S_.wait_ge(sX.h, st["f6_tr"])
                    if stage == 2:
                        dst_ap = w3fm.rearrange("p (sb ic c) -> p ic sb c",
                                                sb=NSUB, ic=8)[:, oc, 4 * hf:4 * hf + 4, :]
                        hnd = S_.activation(dst_ap,
                                            ps_a[:, sl * 512:(sl + 1) * 512]
                                            .rearrange("p (sb c) -> p sb c", sb=4),
                                            af, scale=scol[:, oc:oc + 1])
                    else:
                        hnd = S_.activation(dst_buf[:, oc * TFF + hf * 512:
                                                    oc * TFF + (hf + 1) * 512],
                                            ps_a[:, sl * 512:(sl + 1) * 512],
                                            af, scale=scol[:, oc:oc + 1])
                    st[f"fsl{sl}"] = st[f"f3_ev{stage}"] = sS.inc(hnd)

        for sub in range(NSUB):
            X.wait_ge(sS.h, st["f3_ev2"])
            if st.get("f6_cp") is not None:
                X.wait_ge(sV.h, st["f6_cp"])
            X.dma_start_transpose(
                xhat_fm[:, 0:E].rearrange("p (ic c) -> p ic c", ic=8),
                w3fm[:, sub * E:(sub + 1) * E]
                .rearrange("p (ic c) -> p ic c", ic=8)).then_inc(sX.h, 16)
            sX.n += 16
            st["f6_tr"] = sX.n
            if st.get("f6_add") is not None:
                S_.wait_ge(sV.h, st["f6_add"])
            if sub == 0:
                S_.wait_ge(sSC.h, st["f1_x1st"])
            off = (ft * TFF + sub * 128) * E
            S_.dma_start(f32c[:, :], bass.AP(x1sc, off, [[E, 128], [1, E]])).then_inc(sDS.h, 16)
            sDS.n += 16
            st["f6_r"] = sDS.n
            V.wait_ge(sX.h, st["f6_tr"])
            if st.get("f6_st") is not None:
                V.wait_ge(sOUT.h, st["f6_st"])
            st["f6_cp"] = sV.inc(V.tensor_copy(f32b[:, :], xhat_fm[:, 0:E]))
            emit_ln(f32b[:, :], f32b[:, :], pre_wait_sv=st["f6_cp"])
            V.wait_ge(sV.h, st["ln_out"])
            V.wait_ge(sDS.h, st["f6_r"])
            st["f6_add"] = sV.inc(
                V.tensor_tensor(f32b[:, :], f32b[:, :], f32c[:, :], ALU.add))
            S_.wait_ge(sV.h, st["f6_add"])
            tok = ft * TFF + sub * 128
            S_.dma_start(out_d[tok // S, tok % S:tok % S + 128, :],
                         f32b[:, :]).then_inc(sOUT.h, 16)
            sOUT.n += 16
            st["f6_st"] = sOUT.n

    S_.wait_ge(sOUT.h, sOUT.n)

    def replay(eng, rec):
        for ent in rec.calls:
            hnd = getattr(eng, ent["op"])(*ent["a"], **ent["k"])
            for (semh, val) in ent["chain"]:
                hnd = hnd.then_inc(semh, val)

    with nc.Block() as block:
        @block.tensor
        def _(t):
            replay(t, T)

        @block.vector
        def _(v):
            replay(v, V)

        @block.scalar
        def _(s2):
            replay(s2, S_)

        @block.sync
        def _(x2):
            replay(x2, X)

        @block.gpsimd
        def _(g2):
            replay(g2, G)

    for cm in cms[::-1]:
        cm.__exit__(None, None, None)
    for sm in (sT, sV, sS, sG, sX, sDW, sDS, sSC, sOUT):
        sm._cm.__exit__(None, None, None)
    return nc


def prep_inputs(L0_attn_x, L0_attn_Wq, L0_attn_bq, L0_attn_Wv, L0_attn_bv,
                L0_attn_hyperplanes, L0_ffn_w1_w, L0_ffn_w1_s, L0_ffn_w2_w, L0_ffn_w2_s,
                L0_ffn_h1_w, L0_ffn_h1_s, L0_ffn_h2_w, L0_ffn_h2_s, L0_ffn_w3_w, L0_ffn_w3_s):
    def f(a):
        return np.ascontiguousarray(np.asarray(a, dtype=np.float32))
    def tile_in(wT):
        K_, M_ = wT.shape
        return np.ascontiguousarray(
            wT.reshape(K_ // 128, 128, M_ // 128, 128).transpose(2, 1, 0, 3))

    def tile_qv(wT):
        return np.ascontiguousarray(wT.reshape(8, 128, E).transpose(1, 0, 2))

    com = dict(
        wqT=f(tile_qv(np.asarray(L0_attn_Wq, dtype=np.float32).T)),
        wvT=f(tile_qv(np.asarray(L0_attn_Wv, dtype=np.float32).T)),
        bq=f(L0_attn_bq), bv=f(L0_attn_bv), hyp=f(L0_attn_hyperplanes),
        w1T=f(tile_in(np.asarray(L0_ffn_w1_w, dtype=np.float32).T)),
        w2T=f(tile_in(np.asarray(L0_ffn_w2_w, dtype=np.float32).T)),
        h1T=f(tile_in(np.asarray(L0_ffn_h1_w, dtype=np.float32).T)),
        h2T=f(tile_in(np.asarray(L0_ffn_h2_w, dtype=np.float32).T)),
        w3T=f(tile_in(np.asarray(L0_ffn_w3_w, dtype=np.float32).T)),
        s1=f(L0_ffn_w1_s), s2=f(L0_ffn_w2_s), h1s=f(L0_ffn_h1_s), h2s=f(L0_ffn_h2_s),
        w3s=f(L0_ffn_w3_s))
    x = f(L0_attn_x)
    return x, com


def kernel(**inputs):
    x, com = prep_inputs(**inputs)
    B, S, _ = x.shape
    ncore = 8
    nseq = B // ncore
    nc = build_program(nseq, S)
    in_maps = []
    for c in range(ncore):
        m = dict(com)
        m["x"] = np.ascontiguousarray(x[c * nseq:(c + 1) * nseq])
        in_maps.append(m)
    res = run_bass_kernel_spmd(nc, in_maps, core_ids=list(range(ncore)))
    out = np.concatenate([np.asarray(r["out"]) for r in res.results], axis=0)
    return out.reshape(B, S, E)
